# version 11 (final)
# baseline (speedup 1.0000x reference)
import math
from contextlib import ExitStack

import numpy as np
import ml_dtypes

import concourse.bass as bass
import concourse.mybir as mybir
from concourse.bass_core_rust import IndirectOffsetOnAxis
from concourse.bass_utils import run_bass_kernel_spmd

F32 = mybir.dt.float32
BF16 = mybir.dt.bfloat16
I32 = mybir.dt.int32
AF = mybir.ActivationFunctionType
ALU = mybir.AluOpType

P = 128
S = 4096
D = 4096
NH = 32
HD = 128
TOK = 1024
NTT = TOK // P
NCH = D // P
QB = 512
NQB = S // QB
NU = 8
SC = float(HD) ** -0.5
EPS = 1e-5
GROUPS = [[0, 1, 2, 3], [4, 5, 6, 7]]
NEG = -1.0e30
QKROWS = NH * P + P

TILES = []
_nd = _d = 0
for _u in range(NU):
    for _q in range(NQB):
        for _kt in range(4 * _q + 4):
            dg = _kt >= 4 * _q
            TILES.append((_u, _q, _kt, dg, None if dg else _nd, _d if dg else None))
            if dg:
                _d += 1
            else:
                _nd += 1
NTILE = len(TILES)
N_ND, N_D = _nd, _d
QB_LIST = [(u, q) for u in range(NU) for q in range(NQB)]
QB_FIRST = {}
_ti = 0
for _u in range(NU):
    for _q in range(NQB):
        QB_FIRST[(_u, _q)] = _ti
        _ti += 4 * _q + 4

HORD = [h for u_ in range(8) for h in range(u_, NH, 8)]
POS = {h: i for i, h in enumerate(HORD)}


def _alibi_slopes(n_heads):
    closest_pow2 = 2 ** math.floor(math.log2(n_heads))
    base = 2.0 ** (-2.0 ** (-(math.log2(closest_pow2) - 3)))
    slopes = [base ** i for i in range(1, closest_pow2 + 1)]
    if closest_pow2 != n_heads:
        extra_base = 2.0 ** (-2.0 ** (-(math.log2(2 * closest_pow2) - 3)))
        n_extra = min(closest_pow2, n_heads - closest_pow2)
        slopes += [extra_base ** i for i in range(1, 2 * n_extra + 1, 2)]
    return np.asarray(slopes, dtype=np.float64)


U_BYTES = 112 * 1024


def build_program(debug=False):
    nc = bass.Bass("TRN2", target_bir_lowering=False, debug=False)

    x_own = nc.dram_tensor("x_own", [TOK, D], F32, kind="ExternalInput").ap()
    wq = nc.dram_tensor("wq", [NH, P, D], F32, kind="ExternalInput").ap()
    wkv = nc.dram_tensor("wkv", [P, NCH * 256], F32, kind="ExternalInput").ap()
    wo = nc.dram_tensor("wo", [8, P, NCH * 512], F32, kind="ExternalInput").ap()
    wt = nc.dram_tensor("wt", [8, P, NCH * 512], F32, kind="ExternalInput").ap()
    at_in = nc.dram_tensor("at_in", [P, NCH * 256], F32, kind="ExternalInput").ap()
    bft = nc.dram_tensor("bft", [8, P, 2 * 512], F32, kind="ExternalInput").ap()
    bias_in = nc.dram_tensor("bias_in", [1, D], F32, kind="ExternalInput").ap()
    coeffs_in = nc.dram_tensor("coeffs_in", [16, 1], F32, kind="ExternalInput").ap()
    scales_in = nc.dram_tensor("scales_in", [16, 1], F32, kind="ExternalInput").ap()
    wsel = nc.dram_tensor("wsel", [16, 2 * P], F32, kind="ExternalInput").ap()
    skmat = nc.dram_tensor("skmat", [P, NU * NQB * 32], F32, kind="ExternalInput").ap()
    dmat = nc.dram_tensor("dmat", [P, 4 * QB], F32, kind="ExternalInput").ap()
    erow = nc.dram_tensor("erow", [NU, QB], F32, kind="ExternalInput").ap()
    dslopes = nc.dram_tensor("dslopes", [P, NU], F32, kind="ExternalInput").ap()
    qidx = nc.dram_tensor("qidx", [P, NU * 4], I32, kind="ExternalInput").ap()
    oidx = nc.dram_tensor("oidx", [P, NCH], I32, kind="ExternalInput").ap()
    y_out = nc.dram_tensor("y_out", [TOK, D], F32, kind="ExternalOutput").ap()

    c_kv = nc.dram_tensor("c_kv", [P, TOK], BF16, kind="Internal").ap()
    c_q1 = nc.dram_tensor("c_q1", [16 * P, TOK], BF16, kind="Internal").ap()
    c_q2 = nc.dram_tensor("c_q2", [16 * P, TOK], BF16, kind="Internal").ap()
    c_v = nc.dram_tensor("c_v", [TOK, HD], BF16, kind="Internal").ap()
    g_kt = nc.dram_tensor("g_kt", [4 * P, TOK], BF16, kind="Internal").ap()
    g_q1 = nc.dram_tensor("g_q1", [4 * 16 * P, TOK], BF16, kind="Internal").ap()
    g_q2 = nc.dram_tensor("g_q2", [4 * 16 * P, TOK], BF16, kind="Internal").ap()
    g_v = nc.dram_tensor("g_v", [4 * TOK, HD], BF16, kind="Internal").ap()
    c_o = nc.dram_tensor("c_o", [NU * P * 4, TOK], BF16, kind="Internal").ap()
    g_o = nc.dram_tensor("g_o", [4 * NU * P * 4, TOK], BF16, kind="Internal").ap()
    x2_scr = nc.dram_tensor("x2_scr", [TOK, D], F32, kind="Internal").ap()
    ao_scr = nc.dram_tensor("ao_scr", [TOK, D], BF16, kind="Internal").ap()
    ff_scr = nc.dram_tensor("ff_scr", [TOK, D], BF16, kind="Internal").ap()

    if debug:
        dbg_q1 = nc.dram_tensor("dbg_q1", [16 * P, TOK], BF16, kind="ExternalOutput").ap()
        dbg_q2 = nc.dram_tensor("dbg_q2", [16 * P, TOK], BF16, kind="ExternalOutput").ap()
        dbg_kv = nc.dram_tensor("dbg_kv", [P, TOK], BF16, kind="ExternalOutput").ap()
        dbg_v = nc.dram_tensor("dbg_v", [TOK, HD], BF16, kind="ExternalOutput").ap()
        dbg_o = nc.dram_tensor("dbg_o", [NU * P * 4, TOK], BF16, kind="ExternalOutput").ap()
        dbg_x2 = nc.dram_tensor("dbg_x2", [TOK, D], F32, kind="ExternalOutput").ap()
        dbg_ao = nc.dram_tensor("dbg_ao", [TOK, D], BF16, kind="ExternalOutput").ap()
        dbg_ff = nc.dram_tensor("dbg_ff", [TOK, D], BF16, kind="ExternalOutput").ap()

    es = ExitStack()
    sbt = lambda name, shape, dt: es.enter_context(nc.sbuf_tensor(name, shape, dt))

    with es:
        U_h = sbt("U", [P, U_BYTES // 2], BF16)
        arena_h = sbt("arena", [P, NCH * TOK], BF16)
        ident = sbt("ident", [P, P], BF16)[:]
        ones1 = sbt("ones1", [1, P], BF16)[:]
        ones128 = sbt("ones128", [P, P], BF16)[:]
        stat = sbt("stat", [P, 64], F32)[:]
        bnA = sbt("bnA", [P, 2 * 48], F32)[:]
        bn1 = sbt("bn1", [P, NTT * 48], F32)[:]
        bn2 = sbt("bn2", [P, NTT * 48], F32)[:]
        qpc = sbt("qpc", [P, 2 * QB], BF16)[:]
        vpc = sbt("vpc", [P, 2 * HD], BF16)[:]
        apc = sbt("apc", [P, 4 * QB], BF16)[:]
        bfr_f = sbt("bfr_f", [1, 2 * QB], F32)[:]
        bfr_b = sbt("bfr_b", [1, 2 * QB], BF16)[:]
        cs_s = sbt("cs_s", [16, 2], F32)[:]
        wv_s = sbt("wv_s", [16, 1], F32)[:]
        wsel_s = sbt("wsel_s", [16, 2 * P], F32)[:]
        wvec = sbt("wvec", [P, 2], F32)[:]
        dsl_s = sbt("dsl_s", [P, NU], F32)[:]
        qidx_s = sbt("qidx_s", [P, NU * 4], I32)[:]
        epsv = sbt("epsv", [P, 1], F32)[:]
        oidx_s = sbt("oidx_s", [P, NCH], I32)[:]

        U = U_h.ap()
        Uf = U.bitcast(F32)
        arena = arena_h.ap()

        def ub(off, n):
            return U[:, off // 2: off // 2 + n]

        def uf(off, n):
            return Uf[:, off // 4: off // 4 + n]

        A_x = lambda s: uf(s * 16384, D)
        A_wqf = lambda s: uf(32768 + s * 8192, 2048)
        A_wqb = lambda s: ub(49152 + s * 8192, D)
        A_wkvb = ub(65536, NCH * 256)
        A_lnb = lambda s: ub(81920 + s * 8192, D)

        B_kt = ub(0, S)
        B_v = ub(8192, S)
        B_skm = uf(16384, NU * NQB * 32)
        B_dm = uf(24576, 4 * QB)
        B_ehf = uf(32768, QB)
        B_ehb = ub(34816, QB)
        B_dh = uf(36864, 4 * QB)
        B_pt = lambda s: ub(45056 + s * 2048, 2 * QB)
        B_pb = lambda s: ub(49152 + s * 2048, 2 * QB)
        B_tt = lambda s: uf(55296 + s * 4096, 2 * QB)
        B_ot = lambda s: ub(63488 + s * 1024, QB)
        B_rc = lambda s: uf(65536 + s * 2048, QB)
        B_lns = uf(69632, QB)

        C_stg = lambda s: uf(s * 8192, 2048)
        C_wb = lambda s: ub(16384 + s * 32768, NCH * 512)
        C_x = lambda s: uf(s * 16384, D)
        C_tmp = uf(2 * 16384, D)
        C_o = lambda s: uf((3 + s) * 16384, D)
        C_ao = lambda s: ub(81920 + s * 8192, D)
        C_lnb = lambda s: ub(98304 + s * 8192, D)

        C3_at = ub(81920, NCH * 256)
        C3_bft = lambda s: ub(98304 + s * 2048, 1024)
        C3_xa = ub(102400, 2 * TOK)
        C3_bfs = lambda s: uf(106496 + s * 4096, 1024)

        ppk = [es.enter_context(nc.psum_tensor(f"ppk{i}", [P, 2 * QB], F32)) for i in range(2)]
        psk = [es.enter_context(nc.psum_tensor(f"psk{i}", [P, QB], F32)) for i in range(4)]
        pp = [t[:] for t in ppk]
        pb = [ppk[0][:][:, 0:QB], ppk[0][:][:, QB:2 * QB],
              ppk[1][:][:, 0:QB], ppk[1][:][:, QB:2 * QB],
              psk[0][:], psk[1][:], psk[2][:], psk[3][:]]
        pT = [psk[0].ap().bitcast(BF16)[:, 0:P], psk[1].ap().bitcast(BF16)[:, 0:P]]

        names = ("s_ld s_x s_mv s_sd s_rs s_ln s_tr s_trc s_wq s_wqb s_qmm s_qev "
                 "s_qdm s_cc s_bl s_eh s_un s_S s_cA s_cD s_pD s_pA s_pf s_rc "
                 "s_ot s_otd s_wod s_wob s_mm s_ev s_bn s_aod s_cx s_cao s_tmp "
                 "s_x2 s_x2w s_mvB s_sdB s_rsB s_la s_c3l s_c3c s_yd s_v1 s_g1 s_qtl s_otl s_wkv s_a1").split()
        SM = {n: es.enter_context(nc.semaphore(n)) for n in names}
        vc = {"v1": 0, "a1": 0}

        def eV(tt):
            return 1 + tt if tt < 4 else 2 + tt

        NE = 10 + 2 * NH

        NBL = 4 + 32 + 2
        PAIRS = []
        _ndp = _dp = 0
        for _u in range(NU):
            for _q in range(NQB):
                first = QB_FIRST[(_u, _q)]
                n = 4 * _q + 4
                for _j in range(n // 2):
                    i0, i1 = first + 2 * _j, first + 2 * _j + 1
                    dg0 = TILES[i0][3]
                    assert dg0 == TILES[i1][3]
                    PAIRS.append((_u * NQB + _q, len(PAIRS), i0, i1, dg0,
                                  None if dg0 else _ndp, _dp if dg0 else None))
                    if dg0:
                        _dp += 1
                    else:
                        _ndp += 1
        NPAIR = len(PAIRS)
        PAIR_OF_TILE = {}
        for pr in PAIRS:
            PAIR_OF_TILE[pr[2]] = pr
            PAIR_OF_TILE[pr[3]] = pr


        def emit_evac(scalar, e):
            scalar.wait_ge(SM["s_qmm"], e + 1)
            if e == 0 or e == 5:
                scalar.activation(out=qpc[:, (e % 2) * QB:(e % 2) * QB + QB],
                                  in_=pb[2], func=AF.Copy).then_inc(SM["s_qev"], 1)
            elif e < 10:
                tt = e - 1 if e < 5 else e - 2
                if tt >= 2:
                    scalar.wait_ge(SM["s_qdm"], 16 * (eV(tt - 2) + 1))
                scalar.activation(out=vpc[:, (tt % 2) * HD:(tt % 2 + 1) * HD],
                                  in_=pb[3][:, 0:HD], func=AF.Copy).then_inc(SM["s_qev"], 1)
            else:
                k = e - 10
                if k >= 2:
                    scalar.wait_ge(SM["s_qdm"], 16 * (e - 1))
                elif k == 0:
                    scalar.wait_ge(SM["s_qdm"], 16 * 1)
                else:
                    scalar.wait_ge(SM["s_qdm"], 16 * 6)
                scalar.activation(out=qpc[:, (e % 2) * QB:(e % 2) * QB + QB],
                                  in_=pb[k % 4], func=AF.Copy).then_inc(SM["s_qev"], 1)

        with nc.Block("A") as blk:

            @blk.sync
            def _(sync):
                for i in range(2):
                    sync.dma_start(out=A_wqf(i), in_=wkv[:, i * 2048:(i + 1) * 2048]
                                   ).then_inc(SM["s_ld"], 16)
                sync.dma_start(out=cs_s[:, 0:1], in_=coeffs_in).then_inc(SM["s_ld"], 16)
                sync.dma_start(out=cs_s[:, 1:2], in_=scales_in).then_inc(SM["s_ld"], 16)
                sync.dma_start(out=wsel_s, in_=wsel).then_inc(SM["s_ld"], 16)
                sync.dma_start(out=dsl_s, in_=dslopes).then_inc(SM["s_ld"], 16)
                sync.dma_start(out=qidx_s, in_=qidx).then_inc(SM["s_ld"], 16)
                sync.dma_start(out=oidx_s, in_=oidx).then_inc(SM["s_ld"], 16)
                for i in range(2, 4):
                    sync.wait_ge(SM["s_wkv"], i - 1)
                    sync.wait_ge(SM["s_ld"], 16 * (6 + i))
                    sync.dma_start(out=A_wqf(i % 2), in_=wkv[:, i * 2048:(i + 1) * 2048]
                                   ).then_inc(SM["s_ld"], 16)
                for tt in range(NTT):
                    if tt >= 2:
                        sync.wait_ge(SM["s_ln"], tt - 1)
                    if tt >= 1:
                        sync.wait_ge(SM["s_x"], 16 * tt)
                    sync.dma_start(out=A_x(tt % 2),
                                   in_=x_own[tt * P:(tt + 1) * P, :]).then_inc(SM["s_x"], 16)
                for hi in range(NH):
                    h = HORD[hi]
                    for half in range(2):
                        hh = 2 * hi + half
                        if hh >= 2:
                            sync.wait_ge(SM["s_wqb"], hh - 1)
                        else:
                            sync.wait_ge(SM["s_wkv"], 4)
                        if hh >= 1:
                            sync.wait_ge(SM["s_wq"], 16 * hh)
                        sync.dma_start(out=A_wqf(hh % 2),
                                       in_=wq[h][:, half * 2048:(half + 1) * 2048]
                                       ).then_inc(SM["s_wq"], 16)
                sync.wait_ge(SM["s_ld"], 16 * 10)
                sync.wait_ge(SM["s_ln"], 8)
                sync.wait_ge(SM["s_cc"], 1)
                for r in range(4):
                    sync.dma_start(out=B_kt[:, r * TOK:(r + 1) * TOK],
                                   in_=g_kt[r * P:(r + 1) * P, :]).then_inc(SM["s_bl"], 16)
                sync.wait_ge(SM["s_cc"], 2)
                for r in range(4):
                    for tt in range(NTT):
                        sync.dma_start(out=B_v[:, (r * NTT + tt) * HD:(r * NTT + tt + 1) * HD],
                                       in_=g_v[r * TOK + tt * P: r * TOK + (tt + 1) * P, :]
                                       ).then_inc(SM["s_bl"], 16)
                sync.dma_start(out=B_skm, in_=skmat).then_inc(SM["s_bl"], 16)
                sync.dma_start(out=B_dm, in_=dmat).then_inc(SM["s_bl"], 16)
                sync.wait_ge(SM["s_wqb"], 64)
                for u in range(NU):
                    if u >= 1:
                        sync.wait_ge(SM["s_un"], u)
                        sync.wait_ge(SM["s_eh"], 16 * u)
                    sync.dma_start(out=B_ehf,
                                   in_=erow[u:u + 1, :].partition_broadcast(P)
                                   ).then_inc(SM["s_eh"], 16)

            @blk.gpsimd
            def _(gpsimd):
                gpsimd.memset(ident, 0.0).then_inc(SM["s_g1"], 1)
                gpsimd.wait_ge(SM["s_g1"], 1)
                gpsimd.affine_select(out=ident, in_=ident, compare_op=ALU.not_equal,
                                     fill=1.0, base=0, pattern=[[-1, P]], channel_multiplier=1
                                     ).then_inc(SM["s_g1"], 1)
                gpsimd.memset(ones1, 1.0)
                gpsimd.memset(epsv, EPS)
                gpsimd.memset(ones128, 1.0).then_inc(SM["s_g1"], 1)

                def contrib(e):
                    gpsimd.wait_ge(SM["s_qev"], e + 1)
                    if e >= 1:
                        gpsimd.wait_ge(SM["s_qdm"], 16 * e)
                    if e == 0 or e == 5:
                        tb = 0 if e == 0 else 1
                        gpsimd.dma_start(out=c_kv[:, tb * QB:(tb + 1) * QB],
                                         in_=qpc[:, (e % 2) * QB:(e % 2) * QB + QB]
                                         ).then_inc(SM["s_qdm"], 16)
                    elif e < 10:
                        tt = e - 1 if e < 5 else e - 2
                        gpsimd.dma_start(out=c_v[tt * P:(tt + 1) * P, :],
                                         in_=vpc[:, (tt % 2) * HD:(tt % 2 + 1) * HD]
                                         ).then_inc(SM["s_qdm"], 16)
                    else:
                        k = e - 10
                        hi, tb = divmod(k, 2)
                        tgt = c_q1 if hi < 16 else c_q2
                        hr = hi if hi < 16 else hi - 16
                        gpsimd.dma_start(out=tgt[hr * P:(hr + 1) * P, tb * QB:(tb + 1) * QB],
                                         in_=qpc[:, (e % 2) * QB:(e % 2) * QB + QB]
                                         ).then_inc(SM["s_qdm"], 16)

                for e in range(10):
                    contrib(e)
                gpsimd.wait_ge(SM["s_qdm"], 16 * 10)
                gpsimd.collective_compute("AllGather", ALU.bypass, replica_groups=GROUPS,
                                          ins=[c_kv.opt()], outs=[g_kt.opt()]).then_inc(SM["s_cc"])
                gpsimd.collective_compute("AllGather", ALU.bypass, replica_groups=GROUPS,
                                          ins=[c_v.opt()], outs=[g_v.opt()]).then_inc(SM["s_cc"])
                for e in range(10, 42):
                    contrib(e)
                gpsimd.wait_ge(SM["s_qdm"], 16 * 42)
                gpsimd.collective_compute("AllGather", ALU.bypass, replica_groups=GROUPS,
                                          ins=[c_q1.opt()], outs=[g_q1.opt()]).then_inc(SM["s_cc"])
                for e in range(42, NE):
                    contrib(e)
                gpsimd.wait_ge(SM["s_qdm"], 16 * NE)
                gpsimd.collective_compute("AllGather", ALU.bypass, replica_groups=GROUPS,
                                          ins=[c_q2.opt()], outs=[g_q2.opt()]).then_inc(SM["s_cc"])
                if debug:
                    gpsimd.dma_start(out=dbg_q1, in_=c_q1).then_inc(SM["s_qdm"], 16)
                    gpsimd.dma_start(out=dbg_q2, in_=c_q2).then_inc(SM["s_qdm"], 16)
                    gpsimd.dma_start(out=dbg_kv, in_=c_kv).then_inc(SM["s_qdm"], 16)
                    gpsimd.dma_start(out=dbg_v, in_=c_v).then_inc(SM["s_qdm"], 16)
                    gpsimd.wait_ge(SM["s_qdm"], 16 * (NE + 4))
                gpsimd.wait_ge(SM["s_cc"], 3)
                for u in range(NU):
                    if u == 4:
                        gpsimd.wait_ge(SM["s_cc"], 4)
                    gsrc = g_q1 if u < 4 else g_q2
                    for r in range(4):
                        if u * 4 + r >= 1:
                            gpsimd.wait_ge(SM["s_qtl"], 16 * (u * 4 + r))
                        gpsimd.indirect_dma_start(
                            arena[:, u * S + r * TOK: u * S + (r + 1) * TOK], None,
                            gsrc, IndirectOffsetOnAxis(ap=qidx_s[:, u * 4 + r: u * 4 + r + 1], axis=0),
                        ).then_inc(SM["s_qtl"], 16)
                for qg, (u, q) in enumerate(QB_LIST):
                    gpsimd.wait_ge(SM["s_ot"], qg + 1)
                    if qg >= 1:
                        gpsimd.wait_ge(SM["s_otd"], 16 * qg)
                    gpsimd.dma_start(
                        out=c_o[0:NU * P * 4, :]
                            .rearrange("(row four) t -> row (four t)", four=4)
                            [u * P:(u + 1) * P, (q // 2) * TOK + (q % 2) * QB:
                             (q // 2) * TOK + (q % 2) * QB + QB],
                        in_=B_ot(qg % 2)).then_inc(SM["s_otd"], 16)
                gpsimd.wait_ge(SM["s_otd"], 16 * len(QB_LIST))
                if debug:
                    gpsimd.dma_start(out=dbg_o, in_=c_o).then_inc(SM["s_otd"], 16)
                    gpsimd.wait_ge(SM["s_otd"], 16 * (len(QB_LIST) + 1))

            @blk.vector
            def _(vector):
                for i in range(4):
                    vector.wait_ge(SM["s_ld"], 128 if i < 2 else 16 * (7 + i))
                    vector.tensor_copy(out=A_wkvb[:, i * 2048:(i + 1) * 2048],
                                       in_=A_wqf(i % 2)).then_inc(SM["s_wkv"], 1)
                for tt in range(NTT):
                    sl = tt % 2
                    vector.wait_ge(SM["s_x"], 16 * (tt + 1))
                    if tt >= 2:
                        vector.wait_ge(SM["s_ln"], tt - 1)
                    for i in range(8):
                        ins = vector.bn_stats(out=bnA[:, sl * 48 + 6 * i: sl * 48 + 6 * (i + 1)],
                                              in_=A_x(sl)[:, 512 * i: 512 * (i + 1)])
                    ins.then_inc(SM["s_v1"], 1)
                    vc["v1"] += 1
                    vector.wait_ge(SM["s_v1"], vc["v1"])
                    vector.bn_aggr(out=stat[:, 8 * sl: 8 * sl + 2],
                                   in_=bnA[:, sl * 48:(sl + 1) * 48]).then_inc(SM["s_mv"], 1)
                    vector.wait_ge(SM["s_sd"], tt + 1)
                    ins = vector.reciprocal(out=stat[:, 8 * sl + 3: 8 * sl + 4],
                                            in_=stat[:, 8 * sl + 2: 8 * sl + 3])
                    ins.then_inc(SM["s_v1"], 1)
                    vc["v1"] += 1
                    vector.wait_ge(SM["s_v1"], vc["v1"])
                    vector.scalar_tensor_tensor(
                        out=stat[:, 8 * sl + 4: 8 * sl + 5], in0=stat[:, 8 * sl: 8 * sl + 1],
                        scalar=-1.0, in1=stat[:, 8 * sl + 3: 8 * sl + 4],
                        op0=ALU.mult, op1=ALU.mult).then_inc(SM["s_rs"], 1)
                    for c in range(NCH):
                        gi = tt * NCH + c
                        vector.wait_ge(SM["s_tr"], gi + 1)
                        vector.tensor_copy(out=arena[:, c * TOK + tt * P: c * TOK + (tt + 1) * P],
                                           in_=pT[gi % 2]).then_inc(SM["s_trc"], 1)
                for h in range(NH):
                    for half in range(2):
                        hh = 2 * h + half
                        vector.wait_ge(SM["s_wq"], 16 * (hh + 1))
                        if half == 0 and h >= 2:
                            vector.wait_ge(SM["s_qmm"], 10 + 2 * (h - 2) + 2)
                        vector.tensor_copy(
                            out=A_wqb(h % 2)[:, half * 2048:(half + 1) * 2048],
                            in_=A_wqf(hh % 2)).then_inc(SM["s_wqb"], 1)
                for u in range(NU):
                    vector.wait_ge(SM["s_eh"], 16 * (u + 1))
                    vector.tensor_copy(out=B_ehb, in_=B_ehf).then_inc(SM["s_un"], 1)
                    if u == 0:
                        vector.wait_ge(SM["s_bl"], 16 * NBL)
                    for o in range(4):
                        ins = vector.tensor_single_scalar(
                            out=B_dh[:, o * QB:(o + 1) * QB], in_=B_dm[:, o * QB:(o + 1) * QB],
                            scalar=dsl_s[:, u:u + 1], op=ALU.mult)
                    ins.then_inc(SM["s_v1"], 1)
                    vc["v1"] += 1
                    vector.wait_ge(SM["s_v1"], vc["v1"])
                    vector.wait_ge(SM["s_un"], u + 1)
                    for q in range(NQB):
                        qg = u * NQB + q
                        first = QB_FIRST[(u, q)]
                        n = 4 * q + 4
                        for k in range(n):
                            i = first + k
                            _, _, kt, dg, ndl, dl = TILES[i]
                            _, gp, i0, i1, _, ndp, dp = PAIR_OF_TILE[i]
                            half = i - i0
                            if dg:
                                vector.wait_ge(SM["s_S"], i + 1)
                                if half == 0 and dp >= 2:
                                    vector.wait_ge(SM["s_pA"], dp - 1)
                                o = kt - 4 * q
                                vector.scalar_tensor_tensor(
                                    out=B_tt(dp % 2)[:, half * QB:(half + 1) * QB],
                                    in0=pb[2 * (gp % 2) + half], scalar=1.0,
                                    in1=B_dh[:, o * QB:(o + 1) * QB],
                                    op0=ALU.mult, op1=ALU.add).then_inc(SM["s_cD"], 1)
                            else:
                                vector.wait_ge(SM["s_cA"], ndp + 1)
                                if i >= 7:
                                    prev = PAIRS[gp - 3]
                                    vector.wait_ge(SM["s_pf"], prev[3] + 1)
                                col = (u * NQB + q) * 32 + kt
                                vector.scalar_tensor_tensor(
                                    out=B_pb(gp % 3)[:, half * QB:(half + 1) * QB],
                                    in0=B_pt(gp % 2)[:, half * QB:(half + 1) * QB],
                                    scalar=B_skm[:, col:col + 1],
                                    in1=B_ehb,
                                    op0=ALU.mult, op1=ALU.mult).then_inc(SM["s_pD"], 1)
                        if qg >= 2:
                            vector.wait_ge(SM["s_otd"], 16 * (qg - 1))
                        vector.wait_ge(SM["s_rc"], qg + 1)
                        vector.tensor_mul(out=B_ot(qg % 2), in0=pb[4 + qg % 2],
                                          in1=B_rc(qg % 2)).then_inc(SM["s_ot"], 1)


            @blk.scalar
            def _(scalar):
                scalar.wait_ge(SM["s_g1"], 3)

                def ln_iter(tt):
                    sl = tt % 2
                    scalar.wait_ge(SM["s_mv"], tt + 1)
                    scalar.activation(out=stat[:, 8 * sl + 2: 8 * sl + 3],
                                      in_=stat[:, 8 * sl + 1: 8 * sl + 2],
                                      func=AF.Sqrt, bias=epsv).then_inc(SM["s_sd"], 1)
                    scalar.wait_ge(SM["s_rs"], tt + 1)
                    if tt >= 2:
                        scalar.wait_ge(SM["s_tr"], 32 * (tt - 1))
                    scalar.activation(out=A_lnb(sl), in_=A_x(sl), func=AF.Identity,
                                      bias=stat[:, 8 * sl + 4: 8 * sl + 5],
                                      scale=stat[:, 8 * sl + 3: 8 * sl + 4]).then_inc(SM["s_ln"], 1)

                for tt in range(4):
                    ln_iter(tt)
                for e in range(0, 5):
                    emit_evac(scalar, e)
                for tt in range(4, NTT):
                    ln_iter(tt)
                for e in range(5, NE):
                    emit_evac(scalar, e)
                scalar.wait_ge(SM["s_bl"], 16 * NBL)
                for (qg, gp, i0, i1, dg, ndp, dp) in PAIRS:
                    first_q = QB_FIRST[QB_LIST[qg]]
                    n_q = 4 * (qg % NQB) + 4
                    if dg:
                        scalar.wait_ge(SM["s_cD"], TILES[i1][5] + 1)
                        if gp >= 3:
                            prev = PAIRS[gp - 3]
                            scalar.wait_ge(SM["s_pf"], prev[3] + 1)
                        ins = scalar.activation(out=B_pb(gp % 3), in_=B_tt(dp % 2),
                                                func=AF.Exp, scale=SC)
                        ins.then_inc(SM["s_pA"], 1)
                        if i1 == first_q + n_q - 1:
                            scalar.wait_ge(SM["s_pf"], first_q + n_q)
                            if qg >= 2:
                                scalar.wait_ge(SM["s_ot"], qg - 1)
                            ins = scalar.activation(out=B_lns, in_=pb[6 + qg % 2],
                                                    func=AF.Ln)
                            ins.then_inc(SM["s_a1"], 1)
                            vc["a1"] += 1
                            scalar.wait_ge(SM["s_a1"], vc["a1"])
                            scalar.activation(out=B_rc(qg % 2), in_=B_lns,
                                              func=AF.Exp, scale=-1.0).then_inc(SM["s_rc"], 1)
                    else:
                        scalar.wait_ge(SM["s_S"], i1 + 1)
                        if ndp >= 2:
                            prevnd = None
                            for pr in PAIRS:
                                if pr[5] == ndp - 2:
                                    prevnd = pr
                                    break
                            scalar.wait_ge(SM["s_pD"], TILES[prevnd[3]][4] + 1)
                        scalar.activation(out=B_pt(gp % 2), in_=pp[gp % 2],
                                          func=AF.Exp, scale=SC).then_inc(SM["s_cA"], 1)


            @blk.tensor
            def _(tensor):
                tensor.wait_ge(SM["s_g1"], 2)
                for tb in range(2):
                    for tloc in range(4):
                        tt = 4 * tb + tloc
                        tensor.wait_ge(SM["s_ln"], tt + 1)
                        for c in range(NCH):
                            gi = tt * NCH + c
                            if gi >= 2:
                                tensor.wait_ge(SM["s_trc"], gi - 1)
                            tensor.transpose(pT[gi % 2],
                                             A_lnb(tt % 2)[:, c * P:(c + 1) * P],
                                             ident).then_inc(SM["s_tr"], 1)
                    tensor.wait_ge(SM["s_trc"], 128 * (tb + 1))
                    tensor.wait_ge(SM["s_wkv"], 4)
                    if tb == 1:
                        tensor.wait_ge(SM["s_qev"], 1)
                    for c in range(NCH):
                        ins = tensor.matmul(pb[2], A_wkvb[:, c * 256: c * 256 + HD],
                                            arena[:, c * TOK + tb * QB: c * TOK + (tb + 1) * QB],
                                            start=(c == 0), stop=(c == NCH - 1))
                    ins.then_inc(SM["s_qmm"], 1)
                    for tloc in range(4):
                        tt = 4 * tb + tloc
                        if tt >= 1:
                            tensor.wait_ge(SM["s_qev"], eV(tt - 1) + 1)
                        for c in range(NCH):
                            ins = tensor.matmul(pb[3][:, 0:HD],
                                                arena[:, c * TOK + tt * P: c * TOK + (tt + 1) * P],
                                                A_wkvb[:, c * 256 + HD: (c + 1) * 256],
                                                start=(c == 0), stop=(c == NCH - 1))
                        ins.then_inc(SM["s_qmm"], 1)
                for h in range(NH):
                    tensor.wait_ge(SM["s_wqb"], 2 * (h + 1))
                    for tb in range(2):
                        k = 2 * h + tb
                        e = 10 + k
                        if k >= 4:
                            tensor.wait_ge(SM["s_qev"], e - 3)
                        elif k == 2:
                            tensor.wait_ge(SM["s_qev"], 6)
                        elif k == 3:
                            tensor.wait_ge(SM["s_qev"], 10)
                        for c in range(NCH):
                            ins = tensor.matmul(pb[k % 4],
                                                A_wqb(h % 2)[:, c * P:(c + 1) * P],
                                                arena[:, c * TOK + tb * QB: c * TOK + (tb + 1) * QB],
                                                start=(c == 0), stop=(c == NCH - 1))
                        ins.then_inc(SM["s_qmm"], 1)
                tensor.wait_ge(SM["s_bl"], 16 * NBL)
                tensor.wait_ge(SM["s_qev"], NE)
                for u in range(NU):
                    tensor.wait_ge(SM["s_qtl"], 16 * 4 * (u + 1))
                    for q in range(NQB):
                        qg = u * NQB + q
                        first = QB_FIRST[(u, q)]
                        n = 4 * q + 4

                        def emit_S(k):
                            i = first + k
                            _, _, kt, _, _, _ = TILES[i]
                            _, gp, i0, i1, dgp, ndpp, dpp = PAIR_OF_TILE[i]
                            half = i - i0
                            if half == 0 and gp >= 2:
                                prev = PAIRS[gp - 2]
                                if prev[4]:
                                    tensor.wait_ge(SM["s_cD"], TILES[prev[3]][5] + 1)
                                else:
                                    tensor.wait_ge(SM["s_cA"], prev[5] + 1)
                            tensor.matmul(pb[2 * (gp % 2) + half], B_kt[:, kt * P:(kt + 1) * P],
                                          arena[:, u * S + q * QB: u * S + (q + 1) * QB],
                                          start=True, stop=True).then_inc(SM["s_S"], 1)

                        emit_S(0)
                        if n >= 2:
                            emit_S(1)
                        if n >= 4:
                            emit_S(2)
                            emit_S(3)
                        for k in range(n):
                            i = first + k
                            _, _, kt, dgp, ndlp, dlp = TILES[i]
                            _, gp, i0, i1, _, ndpp, dpp = PAIR_OF_TILE[i]
                            if dgp:
                                tensor.wait_ge(SM["s_pA"], dpp + 1)
                            else:
                                tensor.wait_ge(SM["s_pD"], ndlp + 1)
                            if k == 0 and qg >= 2:
                                tensor.wait_ge(SM["s_ot"], qg - 1)
                            tensor.matmul(pb[4 + qg % 2], B_v[:, kt * HD:(kt + 1) * HD],
                                          B_pb(gp % 3)[:, (i - i0) * QB:(i - i0 + 1) * QB],
                                          start=(k == 0), stop=(k == n - 1))
                            if k == 0 and qg >= 2:
                                tensor.wait_ge(SM["s_rc"], qg - 1)
                            tensor.matmul(pb[6 + qg % 2], ones128,
                                          B_pb(gp % 3)[:, (i - i0) * QB:(i - i0 + 1) * QB],
                                          start=(k == 0), stop=(k == n - 1)
                                          ).then_inc(SM["s_pf"], 1)
                            if k + 4 < n:
                                emit_S(k + 4)

        with nc.Block("C1m") as blk:

            @blk.sync
            def _(sync):
                sync.wait_ge(SM["s_otd"], 16 * 64)
                for ob in range(8):
                    for pc in range(8):
                        g = 8 * ob + pc
                        if g >= 2:
                            sync.wait_ge(SM["s_wob"], g - 1)
                        if g >= 1:
                            sync.wait_ge(SM["s_wod"], 16 * g)
                        sync.dma_start(out=C_stg(g % 2),
                                       in_=wo[ob][:, pc * 2048:(pc + 1) * 2048]
                                       ).then_inc(SM["s_wod"], 16)

            @blk.gpsimd
            def _(gpsimd):
                gpsimd.collective_compute("AllGather", ALU.bypass, replica_groups=GROUPS,
                                          ins=[c_o.opt()], outs=[g_o.opt()]).then_inc(SM["s_cc"])
                gpsimd.wait_ge(SM["s_cc"], 5)
                for c in range(NCH):
                    gpsimd.indirect_dma_start(
                        arena[:, c * TOK:(c + 1) * TOK], None,
                        g_o, IndirectOffsetOnAxis(ap=oidx_s[:, c:c + 1], axis=0),
                    ).then_inc(SM["s_otl"], 16)
                gpsimd.wait_ge(SM["s_otl"], 16 * NCH)
                for m in range(64):
                    gpsimd.wait_ge(SM["s_ev"], m + 1)
                    if m >= 1:
                        gpsimd.wait_ge(SM["s_aod"], 16 * m)
                    ob_, tt_ = divmod(m, 8)
                    gpsimd.dma_start(out=ao_scr[tt_ * P:(tt_ + 1) * P, ob_ * QB:(ob_ + 1) * QB],
                                     in_=apc[:, (m % 4) * QB:(m % 4 + 1) * QB]
                                     ).then_inc(SM["s_aod"], 16)
                gpsimd.wait_ge(SM["s_aod"], 16 * 64)
                if debug:
                    gpsimd.dma_start(out=dbg_ao, in_=ao_scr).then_inc(SM["s_aod"], 16)
                    gpsimd.wait_ge(SM["s_aod"], 16 * 65)

            @blk.tensor
            def _(tensor):
                tensor.wait_ge(SM["s_otl"], 16 * NCH)
                for m in range(64):
                    ob, tt = divmod(m, 8)
                    if m >= 4:
                        tensor.wait_ge(SM["s_ev"], m - 3)
                    for c in range(NCH):
                        if c % 4 == 0:
                            tensor.wait_ge(SM["s_wob"], 8 * ob + c // 4 + 1)
                        ins = tensor.matmul(pb[m % 4],
                                            arena[:, c * TOK + tt * P: c * TOK + (tt + 1) * P],
                                            C_wb(ob % 2)[:, c * QB:(c + 1) * QB],
                                            start=(c == 0), stop=(c == NCH - 1))
                    ins.then_inc(SM["s_mm"], 1)

            @blk.scalar
            def _(scalar):
                for m in range(64):
                    scalar.wait_ge(SM["s_mm"], m + 1)
                    if m >= 4:
                        scalar.wait_ge(SM["s_aod"], 16 * (m - 3))
                        scalar.wait_ge(SM["s_bn"], m - 3)
                    scalar.activation(out=apc[:, (m % 4) * QB:(m % 4 + 1) * QB],
                                      in_=pb[m % 4], func=AF.Copy).then_inc(SM["s_ev"], 1)

            @blk.vector
            def _(vector):
                def bn(m):
                    ob_, tt_ = divmod(m, 8)
                    vector.wait_ge(SM["s_ev"], m + 1)
                    vector.bn_stats(out=bn1[:, tt_ * 48 + ob_ * 6: tt_ * 48 + ob_ * 6 + 6],
                                    in_=apc[:, (m % 4) * QB:(m % 4 + 1) * QB]).then_inc(SM["s_bn"], 1)

                vector.wait_ge(SM["s_otd"], 16 * 64)
                for ob in range(8):
                    if ob >= 2:
                        for m in range(8 * (ob - 2), 8 * (ob - 2) + 8):
                            bn(m)
                    for pc in range(8):
                        g = 8 * ob + pc
                        vector.wait_ge(SM["s_wod"], 16 * (g + 1))
                        if ob >= 2 and pc == 0:
                            vector.wait_ge(SM["s_mm"], 8 * (ob - 1))
                        vector.tensor_copy(out=C_wb(ob % 2)[:, pc * 2048:(pc + 1) * 2048],
                                           in_=C_stg(g % 2)).then_inc(SM["s_wob"], 1)
                for m in range(48, 64):
                    bn(m)

        with nc.Block("C2") as blk:

            @blk.sync
            def _(sync):
                for tt in range(NTT):
                    if tt >= 2:
                        sync.wait_ge(SM["s_x2"], tt - 1)
                    if tt >= 1:
                        sync.wait_ge(SM["s_cx"], 16 * tt)
                    sync.dma_start(out=C_x(tt % 2),
                                   in_=x_own[tt * P:(tt + 1) * P, :]).then_inc(SM["s_cx"], 16)

            @blk.gpsimd
            def _(gpsimd):
                def x2w(tt):
                    gpsimd.wait_ge(SM["s_x2"], tt + 1)
                    if tt >= 1:
                        gpsimd.wait_ge(SM["s_x2w"], 16 * tt)
                    gpsimd.dma_start(out=x2_scr[tt * P:(tt + 1) * P, :],
                                     in_=C_o(tt % 2)).then_inc(SM["s_x2w"], 16)

                for tt in range(NTT):
                    if tt >= 2:
                        gpsimd.wait_ge(SM["s_tmp"], tt - 1)
                    if tt >= 1:
                        gpsimd.wait_ge(SM["s_cao"], 16 * tt)
                    gpsimd.dma_start(out=C_ao(tt % 2),
                                     in_=ao_scr[tt * P:(tt + 1) * P, :]).then_inc(SM["s_cao"], 16)
                    if tt >= 2:
                        x2w(tt - 2)
                x2w(NTT - 2)
                x2w(NTT - 1)
                gpsimd.wait_ge(SM["s_x2w"], 16 * NTT)
                if debug:
                    gpsimd.dma_start(out=dbg_x2, in_=x2_scr).then_inc(SM["s_x2w"], 16)
                    gpsimd.wait_ge(SM["s_x2w"], 16 * (NTT + 1))

            @blk.vector
            def _(vector):
                for tt in range(NTT):
                    sl = tt % 2
                    if tt >= 2:
                        vector.wait_ge(SM["s_tmp"], tt - 1)
                    vector.bn_aggr(out=stat[:, 16 + 8 * sl: 16 + 8 * sl + 2],
                                   in_=bn1[:, tt * 48:(tt + 1) * 48]).then_inc(SM["s_mvB"], 1)
                    vector.wait_ge(SM["s_sdB"], tt + 1)
                    ins = vector.reciprocal(out=stat[:, 16 + 8 * sl + 3: 16 + 8 * sl + 4],
                                            in_=stat[:, 16 + 8 * sl + 2: 16 + 8 * sl + 3])
                    ins.then_inc(SM["s_v1"], 1)
                    vc["v1"] += 1
                    vector.wait_ge(SM["s_v1"], vc["v1"])
                    vector.scalar_tensor_tensor(
                        out=stat[:, 16 + 8 * sl + 4: 16 + 8 * sl + 5],
                        in0=stat[:, 16 + 8 * sl: 16 + 8 * sl + 1],
                        scalar=-1.0, in1=stat[:, 16 + 8 * sl + 3: 16 + 8 * sl + 4],
                        op0=ALU.mult, op1=ALU.mult).then_inc(SM["s_rsB"], 1)
                    vector.wait_ge(SM["s_tmp"], tt + 1)
                    vector.wait_ge(SM["s_cx"], 16 * (tt + 1))
                    if tt >= 2:
                        vector.wait_ge(SM["s_ln"], NTT + tt - 1)
                        vector.wait_ge(SM["s_x2w"], 16 * (tt - 1))
                    ins = vector.tensor_add(out=C_o(sl), in0=C_tmp, in1=C_x(sl))
                    ins.then_inc(SM["s_x2"], 1)
                    vector.wait_ge(SM["s_x2"], tt + 1)
                    for i in range(8):
                        ins = vector.bn_stats(out=bnA[:, sl * 48 + 6 * i: sl * 48 + 6 * (i + 1)],
                                              in_=C_o(sl)[:, 512 * i: 512 * (i + 1)])
                    ins.then_inc(SM["s_v1"], 1)
                    vc["v1"] += 1
                    vector.wait_ge(SM["s_v1"], vc["v1"])
                    vector.bn_aggr(out=stat[:, 32 + 8 * sl: 32 + 8 * sl + 2],
                                   in_=bnA[:, sl * 48:(sl + 1) * 48]).then_inc(SM["s_mv"], 1)
                    vector.wait_ge(SM["s_sd"], NTT + tt + 1)
                    ins = vector.reciprocal(out=stat[:, 32 + 8 * sl + 3: 32 + 8 * sl + 4],
                                            in_=stat[:, 32 + 8 * sl + 2: 32 + 8 * sl + 3])
                    ins.then_inc(SM["s_v1"], 1)
                    vc["v1"] += 1
                    vector.wait_ge(SM["s_v1"], vc["v1"])
                    vector.scalar_tensor_tensor(
                        out=stat[:, 32 + 8 * sl + 4: 32 + 8 * sl + 5],
                        in0=stat[:, 32 + 8 * sl: 32 + 8 * sl + 1],
                        scalar=-1.0, in1=stat[:, 32 + 8 * sl + 3: 32 + 8 * sl + 4],
                        op0=ALU.mult, op1=ALU.mult).then_inc(SM["s_rs"], 1)
                    for c in range(NCH):
                        gi = 256 + tt * NCH + c
                        vector.wait_ge(SM["s_tr"], gi + 1)
                        vector.tensor_copy(out=arena[:, c * TOK + tt * P: c * TOK + (tt + 1) * P],
                                           in_=pT[gi % 2]).then_inc(SM["s_trc"], 1)

            @blk.scalar
            def _(scalar):
                for tt in range(NTT):
                    sl = tt % 2
                    scalar.wait_ge(SM["s_mvB"], tt + 1)
                    scalar.activation(out=stat[:, 16 + 8 * sl + 2: 16 + 8 * sl + 3],
                                      in_=stat[:, 16 + 8 * sl + 1: 16 + 8 * sl + 2],
                                      func=AF.Sqrt, bias=epsv).then_inc(SM["s_sdB"], 1)
                    scalar.wait_ge(SM["s_rsB"], tt + 1)
                    scalar.wait_ge(SM["s_cao"], 16 * (tt + 1))
                    if tt >= 1:
                        scalar.wait_ge(SM["s_x2"], tt)
                    scalar.activation(out=C_tmp, in_=C_ao(sl), func=AF.Identity,
                                      bias=stat[:, 16 + 8 * sl + 4: 16 + 8 * sl + 5],
                                      scale=stat[:, 16 + 8 * sl + 3: 16 + 8 * sl + 4]
                                      ).then_inc(SM["s_tmp"], 1)
                    scalar.wait_ge(SM["s_mv"], NTT + tt + 1)
                    scalar.activation(out=stat[:, 32 + 8 * sl + 2: 32 + 8 * sl + 3],
                                      in_=stat[:, 32 + 8 * sl + 1: 32 + 8 * sl + 2],
                                      func=AF.Sqrt, bias=epsv).then_inc(SM["s_sd"], 1)
                    scalar.wait_ge(SM["s_rs"], NTT + tt + 1)
                    if tt >= 2:
                        scalar.wait_ge(SM["s_tr"], 256 + 32 * (tt - 1))
                    scalar.activation(out=C_lnb(sl), in_=C_o(sl), func=AF.Identity,
                                      bias=stat[:, 32 + 8 * sl + 4: 32 + 8 * sl + 5],
                                      scale=stat[:, 32 + 8 * sl + 3: 32 + 8 * sl + 4]
                                      ).then_inc(SM["s_ln"], 1)

            @blk.tensor
            def _(tensor):
                for tt in range(NTT):
                    tensor.wait_ge(SM["s_ln"], NTT + tt + 1)
                    for c in range(NCH):
                        gi = 256 + tt * NCH + c
                        tensor.wait_ge(SM["s_trc"], gi - 1)
                        tensor.transpose(pT[gi % 2],
                                         C_lnb(tt % 2)[:, c * P:(c + 1) * P],
                                         ident).then_inc(SM["s_tr"], 1)

        with nc.Block("C3s") as blk:

            @blk.sync
            def _(sync):
                sync.dma_start(out=C_stg(0), in_=at_in[:, 0:2048]).then_inc(SM["s_c3l"], 16)
                sync.dma_start(out=C_stg(1), in_=at_in[:, 2048:4096]).then_inc(SM["s_c3l"], 16)
                sync.wait_ge(SM["s_c3c"], 2)
                sync.wait_ge(SM["s_c3l"], 32)
                sync.dma_start(out=C_stg(0), in_=at_in[:, 4096:6144]).then_inc(SM["s_c3l"], 16)
                sync.dma_start(out=C_stg(1), in_=at_in[:, 6144:8192]).then_inc(SM["s_c3l"], 16)

            @blk.vector
            def _(vector):
                vector.tensor_mul(out=wv_s, in0=cs_s[:, 0:1], in1=cs_s[:, 1:2]
                                  ).then_inc(SM["s_la"], 1)
                for i in range(4):
                    vector.wait_ge(SM["s_c3l"], 32 if i < 2 else 64)
                    vector.tensor_copy(out=C3_at[:, i * 2048:(i + 1) * 2048],
                                       in_=C_stg(i % 2)).then_inc(SM["s_c3c"], 1)

            @blk.tensor
            def _(tensor):
                tensor.wait_ge(SM["s_la"], 1)
                tensor.matmul(pb[6][:, 0:1], wsel_s[:, 0:P], wv_s, start=True, stop=True
                              ).then_inc(SM["s_la"], 1)
                tensor.matmul(pb[7][:, 0:1], wsel_s[:, P:2 * P], wv_s, start=True, stop=True
                              ).then_inc(SM["s_la"], 1)
                for g in range(4):
                    c2, tb2 = divmod(g, 2)
                    tensor.wait_ge(SM["s_c3c"], 4)
                    if g >= 2:
                        tensor.wait_ge(SM["s_ev"], 64 + g - 1)
                    for c in range(NCH):
                        ins = tensor.matmul(
                            pb[2 + g % 2],
                            C3_at[:, c * 256 + c2 * P: c * 256 + (c2 + 1) * P],
                            arena[:, c * TOK + tb2 * QB: c * TOK + (tb2 + 1) * QB],
                            start=(c == 0), stop=(c == NCH - 1))
                    ins.then_inc(SM["s_mm"], 1)

            @blk.scalar
            def _(scalar):
                scalar.wait_ge(SM["s_la"], 3)
                scalar.copy(out=wvec[:, 0:1], in_=pb[6][:, 0:1]).then_inc(SM["s_la"], 1)
                scalar.wait_ge(SM["s_la"], 4)
                scalar.copy(out=wvec[:, 1:2], in_=pb[7][:, 0:1]).then_inc(SM["s_la"], 1)
                for g in range(4):
                    c2, tb2 = divmod(g, 2)
                    scalar.wait_ge(SM["s_mm"], 64 + g + 1)
                    scalar.wait_ge(SM["s_la"], 5)
                    scalar.activation(out=C3_xa[:, c2 * TOK + tb2 * QB: c2 * TOK + (tb2 + 1) * QB],
                                      in_=pb[2 + g % 2], func=AF.Copy,
                                      scale=wvec[:, c2:c2 + 1]).then_inc(SM["s_ev"], 1)

        AODB = 64 + (1 if debug else 0)

        with nc.Block("C3m") as blk:

            @blk.sync
            def _(sync):
                for ob in range(8):
                    if ob >= 2:
                        sync.wait_ge(SM["s_c3c"], 1 + 2 * ob)
                    sync.wait_ge(SM["s_c3l"], 16 * (4 + 2 * ob))
                    sync.dma_start(out=C3_bfs(ob % 2), in_=bft[ob]).then_inc(SM["s_c3l"], 16)
                    if ob >= 2:
                        sync.wait_ge(SM["s_c3c"], 2 + 2 * ob)
                    sync.dma_start(out=bfr_f[0:1, (ob % 2) * QB:(ob % 2 + 1) * QB],
                                   in_=bias_in[0:1, ob * QB:(ob + 1) * QB]
                                   ).then_inc(SM["s_c3l"], 16)
                    for pc in range(8):
                        g = 8 * ob + pc
                        if g >= 2:
                            sync.wait_ge(SM["s_wob"], 64 + g - 1)
                        sync.wait_ge(SM["s_wod"], 16 * (64 + g))
                        sync.dma_start(out=C_stg(g % 2),
                                       in_=wt[ob][:, pc * 2048:(pc + 1) * 2048]
                                       ).then_inc(SM["s_wod"], 16)

            @blk.gpsimd
            def _(gpsimd):
                for m in range(64):
                    gpsimd.wait_ge(SM["s_ev"], 68 + m + 1)
                    gpsimd.wait_ge(SM["s_aod"], 16 * (AODB + m))
                    ob_, tt_ = divmod(m, 8)
                    gpsimd.dma_start(out=ff_scr[tt_ * P:(tt_ + 1) * P, ob_ * QB:(ob_ + 1) * QB],
                                     in_=apc[:, (m % 4) * QB:(m % 4 + 1) * QB]
                                     ).then_inc(SM["s_aod"], 16)
                gpsimd.wait_ge(SM["s_aod"], 16 * (AODB + 64))
                if debug:
                    gpsimd.dma_start(out=dbg_ff, in_=ff_scr).then_inc(SM["s_aod"], 16)
                    gpsimd.wait_ge(SM["s_aod"], 16 * (AODB + 65))

            @blk.tensor
            def _(tensor):
                for m in range(64):
                    ob, tt = divmod(m, 8)
                    if m >= 4:
                        tensor.wait_ge(SM["s_ev"], 68 + m - 3)
                    if tt == 0:
                        tensor.wait_ge(SM["s_c3c"], 6 + 2 * ob)
                    for c in range(NCH):
                        if c % 4 == 0:
                            tensor.wait_ge(SM["s_wob"], 64 + 8 * ob + c // 4 + 1)
                        tensor.matmul(pb[m % 4],
                                      arena[:, c * TOK + tt * P: c * TOK + (tt + 1) * P],
                                      C_wb(ob % 2)[:, c * QB:(c + 1) * QB],
                                      start=(c == 0), stop=False)
                    for c2 in range(2):
                        tensor.matmul(pb[m % 4],
                                      C3_xa[:, c2 * TOK + tt * P: c2 * TOK + (tt + 1) * P],
                                      C3_bft(ob % 2)[:, c2 * QB:(c2 + 1) * QB],
                                      start=False, stop=False)
                    tensor.matmul(pb[m % 4], ones1,
                                  bfr_b[0:1, (ob % 2) * QB:(ob % 2 + 1) * QB],
                                  start=False, stop=True).then_inc(SM["s_mm"], 1)

            @blk.scalar
            def _(scalar):
                for m in range(64):
                    scalar.wait_ge(SM["s_mm"], 68 + m + 1)
                    if m >= 4:
                        scalar.wait_ge(SM["s_aod"], 16 * (AODB + m - 3))
                        scalar.wait_ge(SM["s_bn"], 64 + m - 3)
                    scalar.activation(out=apc[:, (m % 4) * QB:(m % 4 + 1) * QB],
                                      in_=pb[m % 4], func=AF.Copy).then_inc(SM["s_ev"], 1)

            @blk.vector
            def _(vector):
                def bn(m):
                    ob_, tt_ = divmod(m, 8)
                    vector.wait_ge(SM["s_ev"], 68 + m + 1)
                    vector.bn_stats(out=bn2[:, tt_ * 48 + ob_ * 6: tt_ * 48 + ob_ * 6 + 6],
                                    in_=apc[:, (m % 4) * QB:(m % 4 + 1) * QB]).then_inc(SM["s_bn"], 1)

                for ob in range(8):
                    if ob >= 2:
                        for m in range(8 * (ob - 2), 8 * (ob - 2) + 8):
                            bn(m)
                    vector.wait_ge(SM["s_c3l"], 16 * (4 + 2 * ob + 2))
                    if ob >= 2:
                        vector.wait_ge(SM["s_mm"], 68 + 8 * (ob - 1))
                    vector.tensor_copy(out=C3_bft(ob % 2), in_=C3_bfs(ob % 2)
                                       ).then_inc(SM["s_c3c"], 1)
                    vector.tensor_copy(out=bfr_b[0:1, (ob % 2) * QB:(ob % 2 + 1) * QB],
                                       in_=bfr_f[0:1, (ob % 2) * QB:(ob % 2 + 1) * QB]
                                       ).then_inc(SM["s_c3c"], 1)
                    for pc in range(8):
                        g = 8 * ob + pc
                        vector.wait_ge(SM["s_wod"], 16 * (64 + g + 1))
                        if ob >= 2 and pc == 0:
                            vector.wait_ge(SM["s_mm"], 68 + 8 * (ob - 1))
                        vector.tensor_copy(out=C_wb(ob % 2)[:, pc * 2048:(pc + 1) * 2048],
                                           in_=C_stg(g % 2)).then_inc(SM["s_wob"], 1)
                for m in range(48, 64):
                    bn(m)

        with nc.Block("C4") as blk:

            @blk.sync
            def _(sync):
                for tt in range(NTT):
                    if tt >= 2:
                        sync.wait_ge(SM["s_x2"], NTT + tt - 1)
                    sync.wait_ge(SM["s_cx"], 16 * (NTT + tt))
                    sync.dma_start(out=C_x(tt % 2),
                                   in_=x2_scr[tt * P:(tt + 1) * P, :]).then_inc(SM["s_cx"], 16)

            @blk.gpsimd
            def _(gpsimd):
                def ydma(tt):
                    gpsimd.wait_ge(SM["s_x2"], NTT + tt + 1)
                    if tt >= 1:
                        gpsimd.wait_ge(SM["s_yd"], 16 * tt)
                    gpsimd.dma_start(out=y_out[tt * P:(tt + 1) * P, :],
                                     in_=C_o(tt % 2)).then_inc(SM["s_yd"], 16)

                for tt in range(NTT):
                    if tt >= 2:
                        gpsimd.wait_ge(SM["s_tmp"], NTT + tt - 1)
                    gpsimd.wait_ge(SM["s_cao"], 16 * (NTT + tt))
                    gpsimd.dma_start(out=C_ao(tt % 2),
                                     in_=ff_scr[tt * P:(tt + 1) * P, :]).then_inc(SM["s_cao"], 16)
                    if tt >= 2:
                        ydma(tt - 2)
                ydma(NTT - 2)
                ydma(NTT - 1)
                gpsimd.wait_ge(SM["s_yd"], 16 * NTT)

            @blk.vector
            def _(vector):
                for tt in range(NTT):
                    sl = tt % 2
                    if tt >= 2:
                        vector.wait_ge(SM["s_tmp"], NTT + tt - 1)
                    vector.bn_aggr(out=stat[:, 16 + 8 * sl: 16 + 8 * sl + 2],
                                   in_=bn2[:, tt * 48:(tt + 1) * 48]).then_inc(SM["s_mvB"], 1)
                    vector.wait_ge(SM["s_sdB"], NTT + tt + 1)
                    ins = vector.reciprocal(out=stat[:, 16 + 8 * sl + 3: 16 + 8 * sl + 4],
                                            in_=stat[:, 16 + 8 * sl + 2: 16 + 8 * sl + 3])
                    ins.then_inc(SM["s_v1"], 1)
                    vc["v1"] += 1
                    vector.wait_ge(SM["s_v1"], vc["v1"])
                    vector.scalar_tensor_tensor(
                        out=stat[:, 16 + 8 * sl + 4: 16 + 8 * sl + 5],
                        in0=stat[:, 16 + 8 * sl: 16 + 8 * sl + 1],
                        scalar=-1.0, in1=stat[:, 16 + 8 * sl + 3: 16 + 8 * sl + 4],
                        op0=ALU.mult, op1=ALU.mult).then_inc(SM["s_rsB"], 1)
                    vector.wait_ge(SM["s_tmp"], NTT + tt + 1)
                    vector.wait_ge(SM["s_cx"], 16 * (NTT + tt + 1))
                    if tt >= 2:
                        vector.wait_ge(SM["s_yd"], 16 * (tt - 1))
                    vector.tensor_add(out=C_o(sl), in0=C_tmp, in1=C_x(sl)
                                      ).then_inc(SM["s_x2"], 1)

            @blk.scalar
            def _(scalar):
                for tt in range(NTT):
                    sl = tt % 2
                    scalar.wait_ge(SM["s_mvB"], NTT + tt + 1)
                    scalar.activation(out=stat[:, 16 + 8 * sl + 2: 16 + 8 * sl + 3],
                                      in_=stat[:, 16 + 8 * sl + 1: 16 + 8 * sl + 2],
                                      func=AF.Sqrt, bias=epsv).then_inc(SM["s_sdB"], 1)
                    scalar.wait_ge(SM["s_rsB"], NTT + tt + 1)
                    scalar.wait_ge(SM["s_cao"], 16 * (NTT + tt + 1))
                    if tt >= 1:
                        scalar.wait_ge(SM["s_x2"], NTT + tt)
                    scalar.activation(out=C_tmp, in_=C_ao(sl), func=AF.Identity,
                                      bias=stat[:, 16 + 8 * sl + 4: 16 + 8 * sl + 5],
                                      scale=stat[:, 16 + 8 * sl + 3: 16 + 8 * sl + 4]
                                      ).then_inc(SM["s_tmp"], 1)

    return nc


def _stage_inputs(L0_attn_hidden_states, L0_attn_Wq, L0_attn_Wkv, L0_attn_Wo,
                  L0_ffn_W, L0_ffn_bias, L0_ffn_A, L0_ffn_Bf, L0_ffn_coeffs,
                  L0_ffn_scales):
    f32 = np.float32
    X = np.asarray(L0_attn_hidden_states, f32)
    Wq = np.asarray(L0_attn_Wq, f32)
    Wkv = np.asarray(L0_attn_Wkv, f32)
    Wo = np.asarray(L0_attn_Wo, f32)
    W = np.asarray(L0_ffn_W, f32)
    bias = np.asarray(L0_ffn_bias, f32)
    A = np.asarray(L0_ffn_A, f32)
    Bf = np.asarray(L0_ffn_Bf, f32)
    coeffs = np.asarray(L0_ffn_coeffs, f32)
    scales = np.asarray(L0_ffn_scales, f32)

    wq_l = np.ascontiguousarray(
        Wq.reshape(NCH, P, NH, HD).transpose(2, 1, 0, 3).reshape(NH, P, D))
    wkv_l = np.ascontiguousarray(
        Wkv.reshape(NCH, P, 256).transpose(1, 0, 2).reshape(P, NCH * 256))
    wo_l = np.ascontiguousarray(
        Wo.reshape(NCH, P, 8, QB).transpose(2, 1, 0, 3).reshape(8, P, NCH * QB))
    WT = np.ascontiguousarray(W.T)
    wt_l = np.ascontiguousarray(
        WT.reshape(NCH, P, 8, QB).transpose(2, 1, 0, 3).reshape(8, P, NCH * QB))
    AT = np.ascontiguousarray(A.reshape(256, D).T)
    at_l = np.ascontiguousarray(
        AT.reshape(NCH, P, 256).transpose(1, 0, 2).reshape(P, NCH * 256))
    BfT = np.ascontiguousarray(Bf.transpose(0, 2, 1).reshape(256, D))
    bft_l = np.ascontiguousarray(
        BfT.reshape(2, P, 8, QB).transpose(2, 1, 0, 3).reshape(8, P, 2 * QB))

    wsel_np = np.zeros((16, 2 * P), f32)
    for c2 in range(2):
        for e8 in range(8):
            wsel_np[8 * c2 + e8, c2 * P + e8 * 16: c2 * P + (e8 + 1) * 16] = 1.0

    slopes = _alibi_slopes(NH)
    pvec = np.arange(P, dtype=np.float64)
    jvec = np.arange(QB, dtype=np.float64)

    dmat_np = np.zeros((P, 4 * QB), f32)
    for o in range(4):
        rel = (pvec[:, None] + 128 * o) - jvec[None, :]
        m = np.where(pvec[:, None] + 128 * o <= jvec[None, :], rel, NEG)
        dmat_np[:, o * QB:(o + 1) * QB] = m.astype(f32)

    in_maps = []
    for c in range(8):
        b, j = divmod(c, 4)
        g = j
        heads = [8 * g + u for u in range(NU)]
        skm = np.zeros((P, NU * NQB * 32), f32)
        er = np.zeros((NU, QB), f32)
        dsl = np.zeros((P, NU), f32)
        for u, h in enumerate(heads):
            sl = slopes[h]
            er[u] = np.exp(-sl * jvec).astype(f32)
            dsl[:, u] = np.float32(sl / SC)
            for q in range(NQB):
                for kt in range(32):
                    skm[:, (u * NQB + q) * 32 + kt] = np.exp(np.minimum(
                        sl * (kt * 128 + pvec - q * QB), 0.0)).astype(f32)
        qx = np.zeros((P, NU * 4), np.int32)
        for u, h in enumerate(heads):
            hp = POS[h] if POS[h] < 16 else POS[h] - 16
            for r in range(4):
                qx[:, u * 4 + r] = r * 16 * P + hp * P + np.arange(P)
        ox = np.zeros((P, NCH), np.int32)
        for cch in range(NCH):
            r, u = divmod(cch, NU)
            ox[:, cch] = r * (NU * P * 4) + (u * P + np.arange(P)) * 4 + j
        in_maps.append({
            "x_own": np.ascontiguousarray(X[b, j * TOK:(j + 1) * TOK, :]),
            "wq": wq_l, "wkv": wkv_l, "wo": wo_l, "wt": wt_l,
            "at_in": at_l, "bft": bft_l,
            "bias_in": bias.reshape(1, D),
            "coeffs_in": coeffs.reshape(16, 1),
            "scales_in": scales.reshape(16, 1),
            "wsel": wsel_np,
            "skmat": skm, "dmat": dmat_np, "erow": er, "dslopes": dsl,
            "qidx": qx, "oidx": ox,
        })
    return in_maps


_NC_CACHE = {}


def kernel(**inputs):
    in_maps = _stage_inputs(**inputs)
    if "nc" not in _NC_CACHE:
        _NC_CACHE["nc"] = build_program(debug=False)
    nc = _NC_CACHE["nc"]
    res = run_bass_kernel_spmd(nc, in_maps, core_ids=list(range(8)))
    out = np.empty((2, S, D), np.float32)
    for c in range(8):
        b, j = divmod(c, 4)
        out[b, j * TOK:(j + 1) * TOK, :] = res.results[c]["y_out"]
    return out


# version 12
# speedup vs baseline: 1.0181x; 1.0181x over previous
import math
from contextlib import ExitStack

import numpy as np
import ml_dtypes

import concourse.bass as bass
import concourse.mybir as mybir
from concourse.bass_core_rust import IndirectOffsetOnAxis
from concourse.bass_utils import run_bass_kernel_spmd

F32 = mybir.dt.float32
BF16 = mybir.dt.bfloat16
I32 = mybir.dt.int32
AF = mybir.ActivationFunctionType
ALU = mybir.AluOpType

P = 128
S = 4096
D = 4096
NH = 32
HD = 128
TOK = 1024
NTT = TOK // P
NCH = D // P
QB = 512
NQB = S // QB
NU = 8
SC = float(HD) ** -0.5
EPS = 1e-5
GROUPS = [[0, 1, 2, 3], [4, 5, 6, 7]]
NEG = -1.0e30
QKROWS = NH * P + P

TILES = []
_nd = _d = 0
for _u in range(NU):
    for _q in range(NQB):
        for _kt in range(4 * _q + 4):
            dg = _kt >= 4 * _q
            TILES.append((_u, _q, _kt, dg, None if dg else _nd, _d if dg else None))
            if dg:
                _d += 1
            else:
                _nd += 1
NTILE = len(TILES)
N_ND, N_D = _nd, _d
QB_LIST = [(u, q) for u in range(NU) for q in range(NQB)]
QB_FIRST = {}
_ti = 0
for _u in range(NU):
    for _q in range(NQB):
        QB_FIRST[(_u, _q)] = _ti
        _ti += 4 * _q + 4

HORD = [h for u_ in range(8) for h in range(u_, NH, 8)]
POS = {h: i for i, h in enumerate(HORD)}


def _alibi_slopes(n_heads):
    closest_pow2 = 2 ** math.floor(math.log2(n_heads))
    base = 2.0 ** (-2.0 ** (-(math.log2(closest_pow2) - 3)))
    slopes = [base ** i for i in range(1, closest_pow2 + 1)]
    if closest_pow2 != n_heads:
        extra_base = 2.0 ** (-2.0 ** (-(math.log2(2 * closest_pow2) - 3)))
        n_extra = min(closest_pow2, n_heads - closest_pow2)
        slopes += [extra_base ** i for i in range(1, 2 * n_extra + 1, 2)]
    return np.asarray(slopes, dtype=np.float64)


U_BYTES = 112 * 1024


def build_program(debug=False):
    nc = bass.Bass("TRN2", target_bir_lowering=False, debug=False)

    x_own = nc.dram_tensor("x_own", [TOK, D], F32, kind="ExternalInput").ap()
    wq = nc.dram_tensor("wq", [NH, P, D], F32, kind="ExternalInput").ap()
    wkv = nc.dram_tensor("wkv", [P, NCH * 256], F32, kind="ExternalInput").ap()
    wo = nc.dram_tensor("wo", [8, P, NCH * 512], F32, kind="ExternalInput").ap()
    wt = nc.dram_tensor("wt", [8, P, NCH * 512], F32, kind="ExternalInput").ap()
    at_in = nc.dram_tensor("at_in", [P, NCH * 256], F32, kind="ExternalInput").ap()
    bft = nc.dram_tensor("bft", [8, P, 2 * 512], F32, kind="ExternalInput").ap()
    bias_in = nc.dram_tensor("bias_in", [1, D], F32, kind="ExternalInput").ap()
    coeffs_in = nc.dram_tensor("coeffs_in", [16, 1], F32, kind="ExternalInput").ap()
    scales_in = nc.dram_tensor("scales_in", [16, 1], F32, kind="ExternalInput").ap()
    wsel = nc.dram_tensor("wsel", [16, 2 * P], F32, kind="ExternalInput").ap()
    skmat = nc.dram_tensor("skmat", [P, NU * NQB * 32], F32, kind="ExternalInput").ap()
    dmat = nc.dram_tensor("dmat", [P, 4 * QB], F32, kind="ExternalInput").ap()
    erow = nc.dram_tensor("erow", [NU, QB], F32, kind="ExternalInput").ap()
    dslopes = nc.dram_tensor("dslopes", [P, NU], F32, kind="ExternalInput").ap()
    qidx = nc.dram_tensor("qidx", [P, NU * 4], I32, kind="ExternalInput").ap()
    oidx = nc.dram_tensor("oidx", [P, NCH], I32, kind="ExternalInput").ap()
    y_out = nc.dram_tensor("y_out", [TOK, D], F32, kind="ExternalOutput").ap()

    c_kv = nc.dram_tensor("c_kv", [P, TOK], BF16, kind="Internal").ap()
    c_q1 = nc.dram_tensor("c_q1", [16 * P, TOK], BF16, kind="Internal").ap()
    c_q2 = nc.dram_tensor("c_q2", [16 * P, TOK], BF16, kind="Internal").ap()
    c_v = nc.dram_tensor("c_v", [TOK, HD], BF16, kind="Internal").ap()
    g_kt = nc.dram_tensor("g_kt", [4 * P, TOK], BF16, kind="Internal").ap()
    g_q1 = nc.dram_tensor("g_q1", [4 * 16 * P, TOK], BF16, kind="Internal").ap()
    g_q2 = nc.dram_tensor("g_q2", [4 * 16 * P, TOK], BF16, kind="Internal").ap()
    g_v = nc.dram_tensor("g_v", [4 * TOK, HD], BF16, kind="Internal").ap()
    c_o = nc.dram_tensor("c_o", [NU * P * 4, TOK], BF16, kind="Internal").ap()
    g_o = nc.dram_tensor("g_o", [4 * NU * P * 4, TOK], BF16, kind="Internal").ap()
    x2_scr = nc.dram_tensor("x2_scr", [TOK, D], F32, kind="Internal").ap()
    ao_scr = nc.dram_tensor("ao_scr", [TOK, D], BF16, kind="Internal").ap()
    ff_scr = nc.dram_tensor("ff_scr", [TOK, D], BF16, kind="Internal").ap()

    if debug:
        dbg_q1 = nc.dram_tensor("dbg_q1", [16 * P, TOK], BF16, kind="ExternalOutput").ap()
        dbg_q2 = nc.dram_tensor("dbg_q2", [16 * P, TOK], BF16, kind="ExternalOutput").ap()
        dbg_kv = nc.dram_tensor("dbg_kv", [P, TOK], BF16, kind="ExternalOutput").ap()
        dbg_v = nc.dram_tensor("dbg_v", [TOK, HD], BF16, kind="ExternalOutput").ap()
        dbg_o = nc.dram_tensor("dbg_o", [NU * P * 4, TOK], BF16, kind="ExternalOutput").ap()
        dbg_x2 = nc.dram_tensor("dbg_x2", [TOK, D], F32, kind="ExternalOutput").ap()
        dbg_ao = nc.dram_tensor("dbg_ao", [TOK, D], BF16, kind="ExternalOutput").ap()
        dbg_ff = nc.dram_tensor("dbg_ff", [TOK, D], BF16, kind="ExternalOutput").ap()

    es = ExitStack()
    sbt = lambda name, shape, dt: es.enter_context(nc.sbuf_tensor(name, shape, dt))

    with es:
        U_h = sbt("U", [P, U_BYTES // 2], BF16)
        arena_h = sbt("arena", [P, NCH * TOK], BF16)
        ident = sbt("ident", [P, P], BF16)[:]
        ones1 = sbt("ones1", [1, P], BF16)[:]
        ones128 = sbt("ones128", [P, P], BF16)[:]
        stat = sbt("stat", [P, 64], F32)[:]
        bnA = sbt("bnA", [P, 2 * 48], F32)[:]
        bn1 = sbt("bn1", [P, NTT * 48], F32)[:]
        bn2 = sbt("bn2", [P, NTT * 48], F32)[:]
        qpc = sbt("qpc", [P, 2 * QB], BF16)[:]
        vpc = sbt("vpc", [P, 2 * HD], BF16)[:]
        apc = sbt("apc", [P, 4 * QB], BF16)[:]
        bfr_f = sbt("bfr_f", [1, 2 * QB], F32)[:]
        bfr_b = sbt("bfr_b", [1, 2 * QB], BF16)[:]
        cs_s = sbt("cs_s", [16, 2], F32)[:]
        wv_s = sbt("wv_s", [16, 1], F32)[:]
        wsel_s = sbt("wsel_s", [16, 2 * P], F32)[:]
        wvec = sbt("wvec", [P, 2], F32)[:]
        dsl_s = sbt("dsl_s", [P, NU], F32)[:]
        qidx_s = sbt("qidx_s", [P, NU * 4], I32)[:]
        epsv = sbt("epsv", [P, 1], F32)[:]
        oidx_s = sbt("oidx_s", [P, NCH], I32)[:]

        U = U_h.ap()
        Uf = U.bitcast(F32)
        arena = arena_h.ap()

        def ub(off, n):
            return U[:, off // 2: off // 2 + n]

        def uf(off, n):
            return Uf[:, off // 4: off // 4 + n]

        A_x = lambda s: uf(s * 16384, D)
        A_wqf = lambda s: uf(32768 + s * 8192, 2048)
        A_wqb = lambda s: ub(49152 + s * 8192, D)
        A_wkvb = ub(65536, NCH * 256)
        A_lnb = lambda s: ub(81920 + s * 8192, D)

        B_kt = ub(0, S)
        B_v = ub(8192, S)
        B_skm = uf(16384, NU * NQB * 32)
        B_dm = uf(24576, 4 * QB)
        B_ehf = uf(32768, QB)
        B_ehb = ub(34816, QB)
        B_dh = uf(36864, 4 * QB)
        B_pt = lambda s: ub(45056 + s * 2048, 2 * QB)
        B_pb = lambda s: ub(49152 + s * 2048, 2 * QB)
        B_tt = lambda s: uf(55296 + s * 4096, 2 * QB)
        B_ot = lambda s: ub(63488 + s * 1024, QB)
        B_rc = lambda s: uf(65536 + s * 2048, QB)
        B_lns = uf(69632, QB)

        C_stg = lambda s: uf(s * 8192, 2048)
        C_wb = lambda s: ub(16384 + s * 32768, NCH * 512)
        C_x = lambda s: uf(s * 16384, D)
        C_tmp = uf(2 * 16384, D)
        C_o = lambda s: uf((3 + s) * 16384, D)
        C_ao = lambda s: ub(81920 + s * 8192, D)
        C_lnb = lambda s: ub(98304 + s * 8192, D)

        C3_at = ub(81920, NCH * 256)
        C3_bft = lambda s: ub(98304 + s * 2048, 1024)
        C3_xa = ub(102400, 2 * TOK)
        C3_bfs = lambda s: uf(106496 + s * 4096, 1024)

        ppk = [es.enter_context(nc.psum_tensor(f"ppk{i}", [P, 2 * QB], F32)) for i in range(2)]
        psk = [es.enter_context(nc.psum_tensor(f"psk{i}", [P, QB], F32)) for i in range(4)]
        pp = [t[:] for t in ppk]
        pb = [ppk[0][:][:, 0:QB], ppk[0][:][:, QB:2 * QB],
              ppk[1][:][:, 0:QB], ppk[1][:][:, QB:2 * QB],
              psk[0][:], psk[1][:], psk[2][:], psk[3][:]]
        pT = [psk[0].ap().bitcast(BF16)[:, 0:P], psk[1].ap().bitcast(BF16)[:, 0:P]]

        names = ("s_ld s_x s_mv s_sd s_rs s_ln s_tr s_trc s_wq s_wqb s_qmm s_qev "
                 "s_qdm s_cc s_bl s_eh s_un s_S s_cA s_cD s_pD s_pA s_pf s_rc "
                 "s_ot s_otd s_wod s_wob s_mm s_ev s_bn s_aod s_cx s_cao s_tmp "
                 "s_x2 s_x2w s_mvB s_sdB s_rsB s_la s_c3l s_c3c s_yd s_v1 s_g1 s_qtl s_otl s_wkv s_a1").split()
        SM = {n: es.enter_context(nc.semaphore(n)) for n in names}
        vc = {"v1": 0, "a1": 0}

        def eV(tt):
            return 1 + tt if tt < 4 else 2 + tt

        NE = 10 + 2 * NH

        NBL = 4 + 32 + 2
        PAIRS = []
        _ndp = _dp = 0
        for _u in range(NU):
            for _q in range(NQB):
                first = QB_FIRST[(_u, _q)]
                n = 4 * _q + 4
                for _j in range(n // 2):
                    i0, i1 = first + 2 * _j, first + 2 * _j + 1
                    dg0 = TILES[i0][3]
                    assert dg0 == TILES[i1][3]
                    PAIRS.append((_u * NQB + _q, len(PAIRS), i0, i1, dg0,
                                  None if dg0 else _ndp, _dp if dg0 else None))
                    if dg0:
                        _dp += 1
                    else:
                        _ndp += 1
        NPAIR = len(PAIRS)
        PAIR_OF_TILE = {}
        for pr in PAIRS:
            PAIR_OF_TILE[pr[2]] = pr
            PAIR_OF_TILE[pr[3]] = pr


        def emit_evac(scalar, e):
            scalar.wait_ge(SM["s_qmm"], e + 1)
            if e == 0 or e == 5:
                scalar.activation(out=qpc[:, (e % 2) * QB:(e % 2) * QB + QB],
                                  in_=pb[2], func=AF.Copy).then_inc(SM["s_qev"], 1)
            elif e < 10:
                tt = e - 1 if e < 5 else e - 2
                if tt >= 2:
                    scalar.wait_ge(SM["s_qdm"], 16 * (eV(tt - 2) + 1))
                scalar.activation(out=vpc[:, (tt % 2) * HD:(tt % 2 + 1) * HD],
                                  in_=pb[3][:, 0:HD], func=AF.Copy).then_inc(SM["s_qev"], 1)
            else:
                k = e - 10
                if k >= 2:
                    scalar.wait_ge(SM["s_qdm"], 16 * (e - 1))
                elif k == 0:
                    scalar.wait_ge(SM["s_qdm"], 16 * 1)
                else:
                    scalar.wait_ge(SM["s_qdm"], 16 * 6)
                scalar.activation(out=qpc[:, (e % 2) * QB:(e % 2) * QB + QB],
                                  in_=pb[k % 4], func=AF.Copy).then_inc(SM["s_qev"], 1)

        with nc.Block("A") as blk:

            @blk.sync
            def _(sync):
                for i in range(2):
                    sync.dma_start(out=A_wqf(i), in_=wkv[:, i * 2048:(i + 1) * 2048]
                                   ).then_inc(SM["s_ld"], 16)
                sync.dma_start(out=cs_s[:, 0:1], in_=coeffs_in).then_inc(SM["s_ld"], 16)
                sync.dma_start(out=cs_s[:, 1:2], in_=scales_in).then_inc(SM["s_ld"], 16)
                sync.dma_start(out=wsel_s, in_=wsel).then_inc(SM["s_ld"], 16)
                sync.dma_start(out=dsl_s, in_=dslopes).then_inc(SM["s_ld"], 16)
                sync.dma_start(out=qidx_s, in_=qidx).then_inc(SM["s_ld"], 16)
                sync.dma_start(out=oidx_s, in_=oidx).then_inc(SM["s_ld"], 16)
                for i in range(2, 4):
                    sync.wait_ge(SM["s_wkv"], i - 1)
                    sync.wait_ge(SM["s_ld"], 16 * (6 + i))
                    sync.dma_start(out=A_wqf(i % 2), in_=wkv[:, i * 2048:(i + 1) * 2048]
                                   ).then_inc(SM["s_ld"], 16)
                for tt in range(NTT):
                    if tt >= 2:
                        sync.wait_ge(SM["s_ln"], tt - 1)
                    if tt >= 1:
                        sync.wait_ge(SM["s_x"], 16 * tt)
                    sync.dma_start(out=A_x(tt % 2),
                                   in_=x_own[tt * P:(tt + 1) * P, :]).then_inc(SM["s_x"], 16)
                for hi in range(NH):
                    h = HORD[hi]
                    for half in range(2):
                        hh = 2 * hi + half
                        if hh >= 2:
                            sync.wait_ge(SM["s_wqb"], hh - 1)
                        else:
                            sync.wait_ge(SM["s_wkv"], 4)
                        if hh >= 1:
                            sync.wait_ge(SM["s_wq"], 16 * hh)
                        sync.dma_start(out=A_wqf(hh % 2),
                                       in_=wq[h][:, half * 2048:(half + 1) * 2048]
                                       ).then_inc(SM["s_wq"], 16)
                sync.wait_ge(SM["s_ld"], 16 * 10)
                sync.wait_ge(SM["s_ln"], 8)
                sync.wait_ge(SM["s_cc"], 1)
                for r in range(4):
                    sync.dma_start(out=B_kt[:, r * TOK:(r + 1) * TOK],
                                   in_=g_kt[r * P:(r + 1) * P, :]).then_inc(SM["s_bl"], 16)
                sync.wait_ge(SM["s_cc"], 2)
                for r in range(4):
                    for tt in range(NTT):
                        sync.dma_start(out=B_v[:, (r * NTT + tt) * HD:(r * NTT + tt + 1) * HD],
                                       in_=g_v[r * TOK + tt * P: r * TOK + (tt + 1) * P, :]
                                       ).then_inc(SM["s_bl"], 16)
                sync.dma_start(out=B_skm, in_=skmat).then_inc(SM["s_bl"], 16)
                sync.dma_start(out=B_dm, in_=dmat).then_inc(SM["s_bl"], 16)
                sync.wait_ge(SM["s_wqb"], 64)
                for u in range(NU):
                    if u >= 1:
                        sync.wait_ge(SM["s_un"], u)
                        sync.wait_ge(SM["s_eh"], 16 * u)
                    sync.dma_start(out=B_ehf,
                                   in_=erow[u:u + 1, :].partition_broadcast(P)
                                   ).then_inc(SM["s_eh"], 16)

            @blk.gpsimd
            def _(gpsimd):
                gpsimd.memset(ident, 0.0).then_inc(SM["s_g1"], 1)
                gpsimd.wait_ge(SM["s_g1"], 1)
                gpsimd.affine_select(out=ident, in_=ident, compare_op=ALU.not_equal,
                                     fill=1.0, base=0, pattern=[[-1, P]], channel_multiplier=1
                                     ).then_inc(SM["s_g1"], 1)
                gpsimd.memset(ones1, 1.0)
                gpsimd.memset(epsv, EPS)
                gpsimd.memset(ones128, 1.0).then_inc(SM["s_g1"], 1)

                def contrib(e):
                    gpsimd.wait_ge(SM["s_qev"], e + 1)
                    if e >= 1:
                        gpsimd.wait_ge(SM["s_qdm"], 16 * e)
                    if e == 0 or e == 5:
                        tb = 0 if e == 0 else 1
                        gpsimd.dma_start(out=c_kv[:, tb * QB:(tb + 1) * QB],
                                         in_=qpc[:, (e % 2) * QB:(e % 2) * QB + QB]
                                         ).then_inc(SM["s_qdm"], 16)
                    elif e < 10:
                        tt = e - 1 if e < 5 else e - 2
                        gpsimd.dma_start(out=c_v[tt * P:(tt + 1) * P, :],
                                         in_=vpc[:, (tt % 2) * HD:(tt % 2 + 1) * HD]
                                         ).then_inc(SM["s_qdm"], 16)
                    else:
                        k = e - 10
                        hi, tb = divmod(k, 2)
                        tgt = c_q1 if hi < 16 else c_q2
                        hr = hi if hi < 16 else hi - 16
                        gpsimd.dma_start(out=tgt[hr * P:(hr + 1) * P, tb * QB:(tb + 1) * QB],
                                         in_=qpc[:, (e % 2) * QB:(e % 2) * QB + QB]
                                         ).then_inc(SM["s_qdm"], 16)

                for e in range(10):
                    contrib(e)
                gpsimd.wait_ge(SM["s_qdm"], 16 * 10)
                gpsimd.collective_compute("AllGather", ALU.bypass, replica_groups=GROUPS,
                                          ins=[c_kv.opt()], outs=[g_kt.opt()]).then_inc(SM["s_cc"])
                gpsimd.collective_compute("AllGather", ALU.bypass, replica_groups=GROUPS,
                                          ins=[c_v.opt()], outs=[g_v.opt()]).then_inc(SM["s_cc"])
                for e in range(10, 42):
                    contrib(e)
                gpsimd.wait_ge(SM["s_qdm"], 16 * 42)
                gpsimd.collective_compute("AllGather", ALU.bypass, replica_groups=GROUPS,
                                          ins=[c_q1.opt()], outs=[g_q1.opt()]).then_inc(SM["s_cc"])
                for e in range(42, NE):
                    contrib(e)
                gpsimd.wait_ge(SM["s_qdm"], 16 * NE)
                gpsimd.collective_compute("AllGather", ALU.bypass, replica_groups=GROUPS,
                                          ins=[c_q2.opt()], outs=[g_q2.opt()]).then_inc(SM["s_cc"])
                if debug:
                    gpsimd.dma_start(out=dbg_q1, in_=c_q1).then_inc(SM["s_qdm"], 16)
                    gpsimd.dma_start(out=dbg_q2, in_=c_q2).then_inc(SM["s_qdm"], 16)
                    gpsimd.dma_start(out=dbg_kv, in_=c_kv).then_inc(SM["s_qdm"], 16)
                    gpsimd.dma_start(out=dbg_v, in_=c_v).then_inc(SM["s_qdm"], 16)
                    gpsimd.wait_ge(SM["s_qdm"], 16 * (NE + 4))
                gpsimd.wait_ge(SM["s_cc"], 3)
                for u in range(NU):
                    if u == 4:
                        gpsimd.wait_ge(SM["s_cc"], 4)
                    gsrc = g_q1 if u < 4 else g_q2
                    for r in range(4):
                        if u * 4 + r >= 1:
                            gpsimd.wait_ge(SM["s_qtl"], 16 * (u * 4 + r))
                        gpsimd.indirect_dma_start(
                            arena[:, u * S + r * TOK: u * S + (r + 1) * TOK], None,
                            gsrc, IndirectOffsetOnAxis(ap=qidx_s[:, u * 4 + r: u * 4 + r + 1], axis=0),
                        ).then_inc(SM["s_qtl"], 16)
                for qg, (u, q) in enumerate(QB_LIST):
                    gpsimd.wait_ge(SM["s_ot"], qg + 1)
                    if qg >= 1:
                        gpsimd.wait_ge(SM["s_otd"], 16 * qg)
                    gpsimd.dma_start(
                        out=c_o[0:NU * P * 4, :]
                            .rearrange("(row four) t -> row (four t)", four=4)
                            [u * P:(u + 1) * P, (q // 2) * TOK + (q % 2) * QB:
                             (q // 2) * TOK + (q % 2) * QB + QB],
                        in_=B_ot(qg % 2)).then_inc(SM["s_otd"], 16)
                gpsimd.wait_ge(SM["s_otd"], 16 * len(QB_LIST))
                if debug:
                    gpsimd.dma_start(out=dbg_o, in_=c_o).then_inc(SM["s_otd"], 16)
                    gpsimd.wait_ge(SM["s_otd"], 16 * (len(QB_LIST) + 1))

            @blk.vector
            def _(vector):
                for i in range(4):
                    vector.wait_ge(SM["s_ld"], 128 if i < 2 else 16 * (7 + i))
                    vector.tensor_copy(out=A_wkvb[:, i * 2048:(i + 1) * 2048],
                                       in_=A_wqf(i % 2)).then_inc(SM["s_wkv"], 1)
                for tt in range(NTT):
                    sl = tt % 2
                    vector.wait_ge(SM["s_x"], 16 * (tt + 1))
                    if tt >= 2:
                        vector.wait_ge(SM["s_ln"], tt - 1)
                    for i in range(8):
                        ins = vector.bn_stats(out=bnA[:, sl * 48 + 6 * i: sl * 48 + 6 * (i + 1)],
                                              in_=A_x(sl)[:, 512 * i: 512 * (i + 1)])
                    ins.then_inc(SM["s_v1"], 1)
                    vc["v1"] += 1
                    vector.wait_ge(SM["s_v1"], vc["v1"])
                    vector.bn_aggr(out=stat[:, 8 * sl: 8 * sl + 2],
                                   in_=bnA[:, sl * 48:(sl + 1) * 48]).then_inc(SM["s_mv"], 1)
                    vector.wait_ge(SM["s_sd"], tt + 1)
                    ins = vector.reciprocal(out=stat[:, 8 * sl + 3: 8 * sl + 4],
                                            in_=stat[:, 8 * sl + 2: 8 * sl + 3])
                    ins.then_inc(SM["s_v1"], 1)
                    vc["v1"] += 1
                    vector.wait_ge(SM["s_v1"], vc["v1"])
                    vector.scalar_tensor_tensor(
                        out=stat[:, 8 * sl + 4: 8 * sl + 5], in0=stat[:, 8 * sl: 8 * sl + 1],
                        scalar=-1.0, in1=stat[:, 8 * sl + 3: 8 * sl + 4],
                        op0=ALU.mult, op1=ALU.mult).then_inc(SM["s_rs"], 1)
                    for c in range(NCH):
                        gi = tt * NCH + c
                        vector.wait_ge(SM["s_tr"], gi + 1)
                        vector.tensor_copy(out=arena[:, c * TOK + tt * P: c * TOK + (tt + 1) * P],
                                           in_=pT[gi % 2]).then_inc(SM["s_trc"], 1)
                for h in range(NH):
                    for half in range(2):
                        hh = 2 * h + half
                        vector.wait_ge(SM["s_wq"], 16 * (hh + 1))
                        if half == 0 and h >= 2:
                            vector.wait_ge(SM["s_qmm"], 10 + 2 * (h - 2) + 2)
                        vector.tensor_copy(
                            out=A_wqb(h % 2)[:, half * 2048:(half + 1) * 2048],
                            in_=A_wqf(hh % 2)).then_inc(SM["s_wqb"], 1)
                for u in range(NU):
                    vector.wait_ge(SM["s_eh"], 16 * (u + 1))
                    vector.tensor_copy(out=B_ehb, in_=B_ehf).then_inc(SM["s_un"], 1)
                    if u == 0:
                        vector.wait_ge(SM["s_bl"], 16 * NBL)
                    for o in range(4):
                        ins = vector.tensor_single_scalar(
                            out=B_dh[:, o * QB:(o + 1) * QB], in_=B_dm[:, o * QB:(o + 1) * QB],
                            scalar=dsl_s[:, u:u + 1], op=ALU.mult)
                    ins.then_inc(SM["s_v1"], 1)
                    vc["v1"] += 1
                    vector.wait_ge(SM["s_v1"], vc["v1"])
                    vector.wait_ge(SM["s_un"], u + 1)
                    for q in range(NQB):
                        qg = u * NQB + q
                        first = QB_FIRST[(u, q)]
                        n = 4 * q + 4
                        for k in range(n):
                            i = first + k
                            _, _, kt, dg, ndl, dl = TILES[i]
                            _, gp, i0, i1, _, ndp, dp = PAIR_OF_TILE[i]
                            half = i - i0
                            if dg:
                                vector.wait_ge(SM["s_S"], i + 1)
                                if half == 0 and dp >= 2:
                                    vector.wait_ge(SM["s_pA"], dp - 1)
                                o = kt - 4 * q
                                vector.scalar_tensor_tensor(
                                    out=B_tt(dp % 2)[:, half * QB:(half + 1) * QB],
                                    in0=pb[2 * (gp % 2) + half], scalar=1.0,
                                    in1=B_dh[:, o * QB:(o + 1) * QB],
                                    op0=ALU.mult, op1=ALU.add).then_inc(SM["s_cD"], 1)
                            else:
                                vector.wait_ge(SM["s_cA"], ndp + 1)
                                if gp >= 3:
                                    vector.wait_ge(SM["s_pf"], gp - 2)
                                col = (u * NQB + q) * 32 + kt
                                vector.scalar_tensor_tensor(
                                    out=B_pb(gp % 3)[:, half * QB:(half + 1) * QB],
                                    in0=B_pt(gp % 2)[:, half * QB:(half + 1) * QB],
                                    scalar=B_skm[:, col:col + 1],
                                    in1=B_ehb,
                                    op0=ALU.mult, op1=ALU.mult).then_inc(SM["s_pD"], 1)
                        if qg >= 2:
                            vector.wait_ge(SM["s_otd"], 16 * (qg - 1))
                        vector.wait_ge(SM["s_rc"], qg + 1)
                        vector.tensor_mul(out=B_ot(qg % 2), in0=pb[4 + qg % 2],
                                          in1=B_rc(qg % 2)).then_inc(SM["s_ot"], 1)


            @blk.scalar
            def _(scalar):
                scalar.wait_ge(SM["s_g1"], 3)

                def ln_iter(tt):
                    sl = tt % 2
                    scalar.wait_ge(SM["s_mv"], tt + 1)
                    scalar.activation(out=stat[:, 8 * sl + 2: 8 * sl + 3],
                                      in_=stat[:, 8 * sl + 1: 8 * sl + 2],
                                      func=AF.Sqrt, bias=epsv).then_inc(SM["s_sd"], 1)
                    scalar.wait_ge(SM["s_rs"], tt + 1)
                    if tt >= 2:
                        scalar.wait_ge(SM["s_tr"], 32 * (tt - 1))
                    scalar.activation(out=A_lnb(sl), in_=A_x(sl), func=AF.Identity,
                                      bias=stat[:, 8 * sl + 4: 8 * sl + 5],
                                      scale=stat[:, 8 * sl + 3: 8 * sl + 4]).then_inc(SM["s_ln"], 1)

                for tt in range(4):
                    ln_iter(tt)
                for e in range(0, 5):
                    emit_evac(scalar, e)
                for tt in range(4, NTT):
                    ln_iter(tt)
                for e in range(5, NE):
                    emit_evac(scalar, e)
                scalar.wait_ge(SM["s_bl"], 16 * NBL)
                for (qg, gp, i0, i1, dg, ndp, dp) in PAIRS:
                    first_q = QB_FIRST[QB_LIST[qg]]
                    n_q = 4 * (qg % NQB) + 4
                    if dg:
                        scalar.wait_ge(SM["s_cD"], TILES[i1][5] + 1)
                        if gp >= 3:
                            scalar.wait_ge(SM["s_pf"], gp - 2)
                        ins = scalar.activation(out=B_pb(gp % 3), in_=B_tt(dp % 2),
                                                func=AF.Exp, scale=SC)
                        ins.then_inc(SM["s_pA"], 1)
                        if i1 == first_q + n_q - 1:
                            scalar.wait_ge(SM["s_pf"], (first_q + n_q) // 2)
                            if qg >= 2:
                                scalar.wait_ge(SM["s_ot"], qg - 1)
                            ins = scalar.activation(out=B_lns, in_=pb[6 + qg % 2],
                                                    func=AF.Ln)
                            ins.then_inc(SM["s_a1"], 1)
                            vc["a1"] += 1
                            scalar.wait_ge(SM["s_a1"], vc["a1"])
                            scalar.activation(out=B_rc(qg % 2), in_=B_lns,
                                              func=AF.Exp, scale=-1.0).then_inc(SM["s_rc"], 1)
                    else:
                        scalar.wait_ge(SM["s_S"], i1 + 1)
                        if ndp >= 2:
                            prevnd = None
                            for pr in PAIRS:
                                if pr[5] == ndp - 2:
                                    prevnd = pr
                                    break
                            scalar.wait_ge(SM["s_pD"], TILES[prevnd[3]][4] + 1)
                        scalar.activation(out=B_pt(gp % 2), in_=pp[gp % 2],
                                          func=AF.Exp, scale=SC).then_inc(SM["s_cA"], 1)


            @blk.tensor
            def _(tensor):
                tensor.wait_ge(SM["s_g1"], 2)
                for tb in range(2):
                    for tloc in range(4):
                        tt = 4 * tb + tloc
                        tensor.wait_ge(SM["s_ln"], tt + 1)
                        for c in range(NCH):
                            gi = tt * NCH + c
                            if gi >= 2:
                                tensor.wait_ge(SM["s_trc"], gi - 1)
                            tensor.transpose(pT[gi % 2],
                                             A_lnb(tt % 2)[:, c * P:(c + 1) * P],
                                             ident).then_inc(SM["s_tr"], 1)
                    tensor.wait_ge(SM["s_trc"], 128 * (tb + 1))
                    tensor.wait_ge(SM["s_wkv"], 4)
                    if tb == 1:
                        tensor.wait_ge(SM["s_qev"], 1)
                    for c in range(NCH):
                        ins = tensor.matmul(pb[2], A_wkvb[:, c * 256: c * 256 + HD],
                                            arena[:, c * TOK + tb * QB: c * TOK + (tb + 1) * QB],
                                            start=(c == 0), stop=(c == NCH - 1))
                    ins.then_inc(SM["s_qmm"], 1)
                    for tloc in range(4):
                        tt = 4 * tb + tloc
                        if tt >= 1:
                            tensor.wait_ge(SM["s_qev"], eV(tt - 1) + 1)
                        for c in range(NCH):
                            ins = tensor.matmul(pb[3][:, 0:HD],
                                                arena[:, c * TOK + tt * P: c * TOK + (tt + 1) * P],
                                                A_wkvb[:, c * 256 + HD: (c + 1) * 256],
                                                start=(c == 0), stop=(c == NCH - 1))
                        ins.then_inc(SM["s_qmm"], 1)
                for h in range(NH):
                    tensor.wait_ge(SM["s_wqb"], 2 * (h + 1))
                    for tb in range(2):
                        k = 2 * h + tb
                        e = 10 + k
                        if k >= 4:
                            tensor.wait_ge(SM["s_qev"], e - 3)
                        elif k == 2:
                            tensor.wait_ge(SM["s_qev"], 6)
                        elif k == 3:
                            tensor.wait_ge(SM["s_qev"], 10)
                        for c in range(NCH):
                            ins = tensor.matmul(pb[k % 4],
                                                A_wqb(h % 2)[:, c * P:(c + 1) * P],
                                                arena[:, c * TOK + tb * QB: c * TOK + (tb + 1) * QB],
                                                start=(c == 0), stop=(c == NCH - 1))
                        ins.then_inc(SM["s_qmm"], 1)
                tensor.wait_ge(SM["s_bl"], 16 * NBL)
                tensor.wait_ge(SM["s_qev"], NE)
                for u in range(NU):
                    tensor.wait_ge(SM["s_qtl"], 16 * 4 * (u + 1))
                    for q in range(NQB):
                        qg = u * NQB + q
                        first = QB_FIRST[(u, q)]
                        n = 4 * q + 4

                        def emit_S(k):
                            i = first + k
                            _, _, kt, _, _, _ = TILES[i]
                            _, gp, i0, i1, dgp, ndpp, dpp = PAIR_OF_TILE[i]
                            half = i - i0
                            if half == 0 and gp >= 2:
                                prev = PAIRS[gp - 2]
                                if prev[4]:
                                    tensor.wait_ge(SM["s_cD"], TILES[prev[3]][5] + 1)
                                else:
                                    tensor.wait_ge(SM["s_cA"], prev[5] + 1)
                            tensor.matmul(pb[2 * (gp % 2) + half], B_kt[:, kt * P:(kt + 1) * P],
                                          arena[:, u * S + q * QB: u * S + (q + 1) * QB],
                                          start=True, stop=True).then_inc(SM["s_S"], 1)

                        emit_S(0)
                        if n >= 2:
                            emit_S(1)
                        if n >= 4:
                            emit_S(2)
                            emit_S(3)
                        for k in range(n):
                            i = first + k
                            _, _, kt, dgp, ndlp, dlp = TILES[i]
                            _, gp, i0, i1, _, ndpp, dpp = PAIR_OF_TILE[i]
                            if dgp:
                                tensor.wait_ge(SM["s_pA"], dpp + 1)
                            else:
                                tensor.wait_ge(SM["s_pD"], ndlp + 1)
                            if k == 0 and qg >= 2:
                                tensor.wait_ge(SM["s_ot"], qg - 1)
                            tensor.matmul(pb[4 + qg % 2], B_v[:, kt * HD:(kt + 1) * HD],
                                          B_pb(gp % 3)[:, (i - i0) * QB:(i - i0 + 1) * QB],
                                          start=(k == 0), stop=(k == n - 1))
                            if k == 0 and qg >= 2:
                                tensor.wait_ge(SM["s_rc"], qg - 1)
                            ins = tensor.matmul(pb[6 + qg % 2], ones128,
                                          B_pb(gp % 3)[:, (i - i0) * QB:(i - i0 + 1) * QB],
                                          start=(k == 0), stop=(k == n - 1))
                            if i == i1:
                                ins.then_inc(SM["s_pf"], 1)
                            if k + 4 < n:
                                emit_S(k + 4)

        with nc.Block("C1m") as blk:

            @blk.sync
            def _(sync):
                sync.wait_ge(SM["s_otd"], 16 * 64)
                for ob in range(8):
                    for pc in range(8):
                        g = 8 * ob + pc
                        if g >= 2:
                            sync.wait_ge(SM["s_wob"], g - 1)
                        if g >= 1:
                            sync.wait_ge(SM["s_wod"], 16 * g)
                        sync.dma_start(out=C_stg(g % 2),
                                       in_=wo[ob][:, pc * 2048:(pc + 1) * 2048]
                                       ).then_inc(SM["s_wod"], 16)

            @blk.gpsimd
            def _(gpsimd):
                gpsimd.collective_compute("AllGather", ALU.bypass, replica_groups=GROUPS,
                                          ins=[c_o.opt()], outs=[g_o.opt()]).then_inc(SM["s_cc"])
                gpsimd.wait_ge(SM["s_cc"], 5)
                for c in range(NCH):
                    gpsimd.indirect_dma_start(
                        arena[:, c * TOK:(c + 1) * TOK], None,
                        g_o, IndirectOffsetOnAxis(ap=oidx_s[:, c:c + 1], axis=0),
                    ).then_inc(SM["s_otl"], 16)
                gpsimd.wait_ge(SM["s_otl"], 16 * NCH)
                for m in range(64):
                    gpsimd.wait_ge(SM["s_ev"], m + 1)
                    if m >= 1:
                        gpsimd.wait_ge(SM["s_aod"], 16 * m)
                    ob_, tt_ = divmod(m, 8)
                    gpsimd.dma_start(out=ao_scr[tt_ * P:(tt_ + 1) * P, ob_ * QB:(ob_ + 1) * QB],
                                     in_=apc[:, (m % 4) * QB:(m % 4 + 1) * QB]
                                     ).then_inc(SM["s_aod"], 16)
                gpsimd.wait_ge(SM["s_aod"], 16 * 64)
                if debug:
                    gpsimd.dma_start(out=dbg_ao, in_=ao_scr).then_inc(SM["s_aod"], 16)
                    gpsimd.wait_ge(SM["s_aod"], 16 * 65)

            @blk.tensor
            def _(tensor):
                tensor.wait_ge(SM["s_otl"], 16 * NCH)
                for m in range(64):
                    ob, tt = divmod(m, 8)
                    if m >= 4:
                        tensor.wait_ge(SM["s_ev"], m - 3)
                    for c in range(NCH):
                        if c % 4 == 0:
                            tensor.wait_ge(SM["s_wob"], 8 * ob + c // 4 + 1)
                        ins = tensor.matmul(pb[m % 4],
                                            arena[:, c * TOK + tt * P: c * TOK + (tt + 1) * P],
                                            C_wb(ob % 2)[:, c * QB:(c + 1) * QB],
                                            start=(c == 0), stop=(c == NCH - 1))
                    ins.then_inc(SM["s_mm"], 1)

            @blk.scalar
            def _(scalar):
                for m in range(64):
                    scalar.wait_ge(SM["s_mm"], m + 1)
                    if m >= 4:
                        scalar.wait_ge(SM["s_aod"], 16 * (m - 3))
                        scalar.wait_ge(SM["s_bn"], m - 3)
                    scalar.activation(out=apc[:, (m % 4) * QB:(m % 4 + 1) * QB],
                                      in_=pb[m % 4], func=AF.Copy).then_inc(SM["s_ev"], 1)

            @blk.vector
            def _(vector):
                def bn(m):
                    ob_, tt_ = divmod(m, 8)
                    vector.wait_ge(SM["s_ev"], m + 1)
                    vector.bn_stats(out=bn1[:, tt_ * 48 + ob_ * 6: tt_ * 48 + ob_ * 6 + 6],
                                    in_=apc[:, (m % 4) * QB:(m % 4 + 1) * QB]).then_inc(SM["s_bn"], 1)

                vector.wait_ge(SM["s_otd"], 16 * 64)
                for ob in range(8):
                    if ob >= 2:
                        for m in range(8 * (ob - 2), 8 * (ob - 2) + 8):
                            bn(m)
                    for pc in range(8):
                        g = 8 * ob + pc
                        vector.wait_ge(SM["s_wod"], 16 * (g + 1))
                        if ob >= 2 and pc == 0:
                            vector.wait_ge(SM["s_mm"], 8 * (ob - 1))
                        vector.tensor_copy(out=C_wb(ob % 2)[:, pc * 2048:(pc + 1) * 2048],
                                           in_=C_stg(g % 2)).then_inc(SM["s_wob"], 1)
                for m in range(48, 64):
                    bn(m)

        with nc.Block("C2") as blk:

            @blk.sync
            def _(sync):
                for tt in range(NTT):
                    if tt >= 2:
                        sync.wait_ge(SM["s_x2"], tt - 1)
                    if tt >= 1:
                        sync.wait_ge(SM["s_cx"], 16 * tt)
                    sync.dma_start(out=C_x(tt % 2),
                                   in_=x_own[tt * P:(tt + 1) * P, :]).then_inc(SM["s_cx"], 16)

            @blk.gpsimd
            def _(gpsimd):
                def x2w(tt):
                    gpsimd.wait_ge(SM["s_x2"], tt + 1)
                    if tt >= 1:
                        gpsimd.wait_ge(SM["s_x2w"], 16 * tt)
                    gpsimd.dma_start(out=x2_scr[tt * P:(tt + 1) * P, :],
                                     in_=C_o(tt % 2)).then_inc(SM["s_x2w"], 16)

                for tt in range(NTT):
                    if tt >= 2:
                        gpsimd.wait_ge(SM["s_tmp"], tt - 1)
                    if tt >= 1:
                        gpsimd.wait_ge(SM["s_cao"], 16 * tt)
                    gpsimd.dma_start(out=C_ao(tt % 2),
                                     in_=ao_scr[tt * P:(tt + 1) * P, :]).then_inc(SM["s_cao"], 16)
                    if tt >= 2:
                        x2w(tt - 2)
                x2w(NTT - 2)
                x2w(NTT - 1)
                gpsimd.wait_ge(SM["s_x2w"], 16 * NTT)
                if debug:
                    gpsimd.dma_start(out=dbg_x2, in_=x2_scr).then_inc(SM["s_x2w"], 16)
                    gpsimd.wait_ge(SM["s_x2w"], 16 * (NTT + 1))

            @blk.vector
            def _(vector):
                for tt in range(NTT):
                    sl = tt % 2
                    if tt >= 2:
                        vector.wait_ge(SM["s_tmp"], tt - 1)
                    vector.bn_aggr(out=stat[:, 16 + 8 * sl: 16 + 8 * sl + 2],
                                   in_=bn1[:, tt * 48:(tt + 1) * 48]).then_inc(SM["s_mvB"], 1)
                    vector.wait_ge(SM["s_sdB"], tt + 1)
                    ins = vector.reciprocal(out=stat[:, 16 + 8 * sl + 3: 16 + 8 * sl + 4],
                                            in_=stat[:, 16 + 8 * sl + 2: 16 + 8 * sl + 3])
                    ins.then_inc(SM["s_v1"], 1)
                    vc["v1"] += 1
                    vector.wait_ge(SM["s_v1"], vc["v1"])
                    vector.scalar_tensor_tensor(
                        out=stat[:, 16 + 8 * sl + 4: 16 + 8 * sl + 5],
                        in0=stat[:, 16 + 8 * sl: 16 + 8 * sl + 1],
                        scalar=-1.0, in1=stat[:, 16 + 8 * sl + 3: 16 + 8 * sl + 4],
                        op0=ALU.mult, op1=ALU.mult).then_inc(SM["s_rsB"], 1)
                    vector.wait_ge(SM["s_tmp"], tt + 1)
                    vector.wait_ge(SM["s_cx"], 16 * (tt + 1))
                    if tt >= 2:
                        vector.wait_ge(SM["s_ln"], NTT + tt - 1)
                        vector.wait_ge(SM["s_x2w"], 16 * (tt - 1))
                    ins = vector.tensor_add(out=C_o(sl), in0=C_tmp, in1=C_x(sl))
                    ins.then_inc(SM["s_x2"], 1)
                    vector.wait_ge(SM["s_x2"], tt + 1)
                    for i in range(8):
                        ins = vector.bn_stats(out=bnA[:, sl * 48 + 6 * i: sl * 48 + 6 * (i + 1)],
                                              in_=C_o(sl)[:, 512 * i: 512 * (i + 1)])
                    ins.then_inc(SM["s_v1"], 1)
                    vc["v1"] += 1
                    vector.wait_ge(SM["s_v1"], vc["v1"])
                    vector.bn_aggr(out=stat[:, 32 + 8 * sl: 32 + 8 * sl + 2],
                                   in_=bnA[:, sl * 48:(sl + 1) * 48]).then_inc(SM["s_mv"], 1)
                    vector.wait_ge(SM["s_sd"], NTT + tt + 1)
                    ins = vector.reciprocal(out=stat[:, 32 + 8 * sl + 3: 32 + 8 * sl + 4],
                                            in_=stat[:, 32 + 8 * sl + 2: 32 + 8 * sl + 3])
                    ins.then_inc(SM["s_v1"], 1)
                    vc["v1"] += 1
                    vector.wait_ge(SM["s_v1"], vc["v1"])
                    vector.scalar_tensor_tensor(
                        out=stat[:, 32 + 8 * sl + 4: 32 + 8 * sl + 5],
                        in0=stat[:, 32 + 8 * sl: 32 + 8 * sl + 1],
                        scalar=-1.0, in1=stat[:, 32 + 8 * sl + 3: 32 + 8 * sl + 4],
                        op0=ALU.mult, op1=ALU.mult).then_inc(SM["s_rs"], 1)
                    for c in range(NCH):
                        gi = 256 + tt * NCH + c
                        vector.wait_ge(SM["s_tr"], gi + 1)
                        vector.tensor_copy(out=arena[:, c * TOK + tt * P: c * TOK + (tt + 1) * P],
                                           in_=pT[gi % 2]).then_inc(SM["s_trc"], 1)

            @blk.scalar
            def _(scalar):
                for tt in range(NTT):
                    sl = tt % 2
                    scalar.wait_ge(SM["s_mvB"], tt + 1)
                    scalar.activation(out=stat[:, 16 + 8 * sl + 2: 16 + 8 * sl + 3],
                                      in_=stat[:, 16 + 8 * sl + 1: 16 + 8 * sl + 2],
                                      func=AF.Sqrt, bias=epsv).then_inc(SM["s_sdB"], 1)
                    scalar.wait_ge(SM["s_rsB"], tt + 1)
                    scalar.wait_ge(SM["s_cao"], 16 * (tt + 1))
                    if tt >= 1:
                        scalar.wait_ge(SM["s_x2"], tt)
                    scalar.activation(out=C_tmp, in_=C_ao(sl), func=AF.Identity,
                                      bias=stat[:, 16 + 8 * sl + 4: 16 + 8 * sl + 5],
                                      scale=stat[:, 16 + 8 * sl + 3: 16 + 8 * sl + 4]
                                      ).then_inc(SM["s_tmp"], 1)
                    scalar.wait_ge(SM["s_mv"], NTT + tt + 1)
                    scalar.activation(out=stat[:, 32 + 8 * sl + 2: 32 + 8 * sl + 3],
                                      in_=stat[:, 32 + 8 * sl + 1: 32 + 8 * sl + 2],
                                      func=AF.Sqrt, bias=epsv).then_inc(SM["s_sd"], 1)
                    scalar.wait_ge(SM["s_rs"], NTT + tt + 1)
                    if tt >= 2:
                        scalar.wait_ge(SM["s_tr"], 256 + 32 * (tt - 1))
                    scalar.activation(out=C_lnb(sl), in_=C_o(sl), func=AF.Identity,
                                      bias=stat[:, 32 + 8 * sl + 4: 32 + 8 * sl + 5],
                                      scale=stat[:, 32 + 8 * sl + 3: 32 + 8 * sl + 4]
                                      ).then_inc(SM["s_ln"], 1)

            @blk.tensor
            def _(tensor):
                for tt in range(NTT):
                    tensor.wait_ge(SM["s_ln"], NTT + tt + 1)
                    for c in range(NCH):
                        gi = 256 + tt * NCH + c
                        tensor.wait_ge(SM["s_trc"], gi - 1)
                        tensor.transpose(pT[gi % 2],
                                         C_lnb(tt % 2)[:, c * P:(c + 1) * P],
                                         ident).then_inc(SM["s_tr"], 1)

        with nc.Block("C3s") as blk:

            @blk.sync
            def _(sync):
                sync.dma_start(out=C_stg(0), in_=at_in[:, 0:2048]).then_inc(SM["s_c3l"], 16)
                sync.dma_start(out=C_stg(1), in_=at_in[:, 2048:4096]).then_inc(SM["s_c3l"], 16)
                sync.wait_ge(SM["s_c3c"], 2)
                sync.wait_ge(SM["s_c3l"], 32)
                sync.dma_start(out=C_stg(0), in_=at_in[:, 4096:6144]).then_inc(SM["s_c3l"], 16)
                sync.dma_start(out=C_stg(1), in_=at_in[:, 6144:8192]).then_inc(SM["s_c3l"], 16)

            @blk.vector
            def _(vector):
                vector.tensor_mul(out=wv_s, in0=cs_s[:, 0:1], in1=cs_s[:, 1:2]
                                  ).then_inc(SM["s_la"], 1)
                for i in range(4):
                    vector.wait_ge(SM["s_c3l"], 32 if i < 2 else 64)
                    vector.tensor_copy(out=C3_at[:, i * 2048:(i + 1) * 2048],
                                       in_=C_stg(i % 2)).then_inc(SM["s_c3c"], 1)

            @blk.tensor
            def _(tensor):
                tensor.wait_ge(SM["s_la"], 1)
                tensor.matmul(pb[6][:, 0:1], wsel_s[:, 0:P], wv_s, start=True, stop=True
                              ).then_inc(SM["s_la"], 1)
                tensor.matmul(pb[7][:, 0:1], wsel_s[:, P:2 * P], wv_s, start=True, stop=True
                              ).then_inc(SM["s_la"], 1)
                for g in range(4):
                    c2, tb2 = divmod(g, 2)
                    tensor.wait_ge(SM["s_c3c"], 4)
                    if g >= 2:
                        tensor.wait_ge(SM["s_ev"], 64 + g - 1)
                    for c in range(NCH):
                        ins = tensor.matmul(
                            pb[2 + g % 2],
                            C3_at[:, c * 256 + c2 * P: c * 256 + (c2 + 1) * P],
                            arena[:, c * TOK + tb2 * QB: c * TOK + (tb2 + 1) * QB],
                            start=(c == 0), stop=(c == NCH - 1))
                    ins.then_inc(SM["s_mm"], 1)

            @blk.scalar
            def _(scalar):
                scalar.wait_ge(SM["s_la"], 3)
                scalar.copy(out=wvec[:, 0:1], in_=pb[6][:, 0:1]).then_inc(SM["s_la"], 1)
                scalar.wait_ge(SM["s_la"], 4)
                scalar.copy(out=wvec[:, 1:2], in_=pb[7][:, 0:1]).then_inc(SM["s_la"], 1)
                for g in range(4):
                    c2, tb2 = divmod(g, 2)
                    scalar.wait_ge(SM["s_mm"], 64 + g + 1)
                    scalar.wait_ge(SM["s_la"], 5)
                    scalar.activation(out=C3_xa[:, c2 * TOK + tb2 * QB: c2 * TOK + (tb2 + 1) * QB],
                                      in_=pb[2 + g % 2], func=AF.Copy,
                                      scale=wvec[:, c2:c2 + 1]).then_inc(SM["s_ev"], 1)

        AODB = 64 + (1 if debug else 0)

        with nc.Block("C3m") as blk:

            @blk.sync
            def _(sync):
                for ob in range(8):
                    if ob >= 2:
                        sync.wait_ge(SM["s_c3c"], 1 + 2 * ob)
                    sync.wait_ge(SM["s_c3l"], 16 * (4 + 2 * ob))
                    sync.dma_start(out=C3_bfs(ob % 2), in_=bft[ob]).then_inc(SM["s_c3l"], 16)
                    if ob >= 2:
                        sync.wait_ge(SM["s_c3c"], 2 + 2 * ob)
                    sync.dma_start(out=bfr_f[0:1, (ob % 2) * QB:(ob % 2 + 1) * QB],
                                   in_=bias_in[0:1, ob * QB:(ob + 1) * QB]
                                   ).then_inc(SM["s_c3l"], 16)
                    for pc in range(8):
                        g = 8 * ob + pc
                        if g >= 2:
                            sync.wait_ge(SM["s_wob"], 64 + g - 1)
                        sync.wait_ge(SM["s_wod"], 16 * (64 + g))
                        sync.dma_start(out=C_stg(g % 2),
                                       in_=wt[ob][:, pc * 2048:(pc + 1) * 2048]
                                       ).then_inc(SM["s_wod"], 16)

            @blk.gpsimd
            def _(gpsimd):
                for m in range(64):
                    gpsimd.wait_ge(SM["s_ev"], 68 + m + 1)
                    gpsimd.wait_ge(SM["s_aod"], 16 * (AODB + m))
                    ob_, tt_ = divmod(m, 8)
                    gpsimd.dma_start(out=ff_scr[tt_ * P:(tt_ + 1) * P, ob_ * QB:(ob_ + 1) * QB],
                                     in_=apc[:, (m % 4) * QB:(m % 4 + 1) * QB]
                                     ).then_inc(SM["s_aod"], 16)
                gpsimd.wait_ge(SM["s_aod"], 16 * (AODB + 64))
                if debug:
                    gpsimd.dma_start(out=dbg_ff, in_=ff_scr).then_inc(SM["s_aod"], 16)
                    gpsimd.wait_ge(SM["s_aod"], 16 * (AODB + 65))

            @blk.tensor
            def _(tensor):
                for m in range(64):
                    ob, tt = divmod(m, 8)
                    if m >= 4:
                        tensor.wait_ge(SM["s_ev"], 68 + m - 3)
                    if tt == 0:
                        tensor.wait_ge(SM["s_c3c"], 6 + 2 * ob)
                    for c in range(NCH):
                        if c % 4 == 0:
                            tensor.wait_ge(SM["s_wob"], 64 + 8 * ob + c // 4 + 1)
                        tensor.matmul(pb[m % 4],
                                      arena[:, c * TOK + tt * P: c * TOK + (tt + 1) * P],
                                      C_wb(ob % 2)[:, c * QB:(c + 1) * QB],
                                      start=(c == 0), stop=False)
                    for c2 in range(2):
                        tensor.matmul(pb[m % 4],
                                      C3_xa[:, c2 * TOK + tt * P: c2 * TOK + (tt + 1) * P],
                                      C3_bft(ob % 2)[:, c2 * QB:(c2 + 1) * QB],
                                      start=False, stop=False)
                    tensor.matmul(pb[m % 4], ones1,
                                  bfr_b[0:1, (ob % 2) * QB:(ob % 2 + 1) * QB],
                                  start=False, stop=True).then_inc(SM["s_mm"], 1)

            @blk.scalar
            def _(scalar):
                for m in range(64):
                    scalar.wait_ge(SM["s_mm"], 68 + m + 1)
                    if m >= 4:
                        scalar.wait_ge(SM["s_aod"], 16 * (AODB + m - 3))
                        scalar.wait_ge(SM["s_bn"], 64 + m - 3)
                    scalar.activation(out=apc[:, (m % 4) * QB:(m % 4 + 1) * QB],
                                      in_=pb[m % 4], func=AF.Copy).then_inc(SM["s_ev"], 1)

            @blk.vector
            def _(vector):
                def bn(m):
                    ob_, tt_ = divmod(m, 8)
                    vector.wait_ge(SM["s_ev"], 68 + m + 1)
                    vector.bn_stats(out=bn2[:, tt_ * 48 + ob_ * 6: tt_ * 48 + ob_ * 6 + 6],
                                    in_=apc[:, (m % 4) * QB:(m % 4 + 1) * QB]).then_inc(SM["s_bn"], 1)

                for ob in range(8):
                    if ob >= 2:
                        for m in range(8 * (ob - 2), 8 * (ob - 2) + 8):
                            bn(m)
                    vector.wait_ge(SM["s_c3l"], 16 * (4 + 2 * ob + 2))
                    if ob >= 2:
                        vector.wait_ge(SM["s_mm"], 68 + 8 * (ob - 1))
                    vector.tensor_copy(out=C3_bft(ob % 2), in_=C3_bfs(ob % 2)
                                       ).then_inc(SM["s_c3c"], 1)
                    vector.tensor_copy(out=bfr_b[0:1, (ob % 2) * QB:(ob % 2 + 1) * QB],
                                       in_=bfr_f[0:1, (ob % 2) * QB:(ob % 2 + 1) * QB]
                                       ).then_inc(SM["s_c3c"], 1)
                    for pc in range(8):
                        g = 8 * ob + pc
                        vector.wait_ge(SM["s_wod"], 16 * (64 + g + 1))
                        if ob >= 2 and pc == 0:
                            vector.wait_ge(SM["s_mm"], 68 + 8 * (ob - 1))
                        vector.tensor_copy(out=C_wb(ob % 2)[:, pc * 2048:(pc + 1) * 2048],
                                           in_=C_stg(g % 2)).then_inc(SM["s_wob"], 1)
                for m in range(48, 64):
                    bn(m)

        with nc.Block("C4") as blk:

            @blk.sync
            def _(sync):
                for tt in range(NTT):
                    if tt >= 2:
                        sync.wait_ge(SM["s_x2"], NTT + tt - 1)
                    sync.wait_ge(SM["s_cx"], 16 * (NTT + tt))
                    sync.dma_start(out=C_x(tt % 2),
                                   in_=x2_scr[tt * P:(tt + 1) * P, :]).then_inc(SM["s_cx"], 16)

            @blk.gpsimd
            def _(gpsimd):
                def ydma(tt):
                    gpsimd.wait_ge(SM["s_x2"], NTT + tt + 1)
                    if tt >= 1:
                        gpsimd.wait_ge(SM["s_yd"], 16 * tt)
                    gpsimd.dma_start(out=y_out[tt * P:(tt + 1) * P, :],
                                     in_=C_o(tt % 2)).then_inc(SM["s_yd"], 16)

                for tt in range(NTT):
                    if tt >= 2:
                        gpsimd.wait_ge(SM["s_tmp"], NTT + tt - 1)
                    gpsimd.wait_ge(SM["s_cao"], 16 * (NTT + tt))
                    gpsimd.dma_start(out=C_ao(tt % 2),
                                     in_=ff_scr[tt * P:(tt + 1) * P, :]).then_inc(SM["s_cao"], 16)
                    if tt >= 2:
                        ydma(tt - 2)
                ydma(NTT - 2)
                ydma(NTT - 1)
                gpsimd.wait_ge(SM["s_yd"], 16 * NTT)

            @blk.vector
            def _(vector):
                for tt in range(NTT):
                    sl = tt % 2
                    if tt >= 2:
                        vector.wait_ge(SM["s_tmp"], NTT + tt - 1)
                    vector.bn_aggr(out=stat[:, 16 + 8 * sl: 16 + 8 * sl + 2],
                                   in_=bn2[:, tt * 48:(tt + 1) * 48]).then_inc(SM["s_mvB"], 1)
                    vector.wait_ge(SM["s_sdB"], NTT + tt + 1)
                    ins = vector.reciprocal(out=stat[:, 16 + 8 * sl + 3: 16 + 8 * sl + 4],
                                            in_=stat[:, 16 + 8 * sl + 2: 16 + 8 * sl + 3])
                    ins.then_inc(SM["s_v1"], 1)
                    vc["v1"] += 1
                    vector.wait_ge(SM["s_v1"], vc["v1"])
                    vector.scalar_tensor_tensor(
                        out=stat[:, 16 + 8 * sl + 4: 16 + 8 * sl + 5],
                        in0=stat[:, 16 + 8 * sl: 16 + 8 * sl + 1],
                        scalar=-1.0, in1=stat[:, 16 + 8 * sl + 3: 16 + 8 * sl + 4],
                        op0=ALU.mult, op1=ALU.mult).then_inc(SM["s_rsB"], 1)
                    vector.wait_ge(SM["s_tmp"], NTT + tt + 1)
                    vector.wait_ge(SM["s_cx"], 16 * (NTT + tt + 1))
                    if tt >= 2:
                        vector.wait_ge(SM["s_yd"], 16 * (tt - 1))
                    vector.tensor_add(out=C_o(sl), in0=C_tmp, in1=C_x(sl)
                                      ).then_inc(SM["s_x2"], 1)

            @blk.scalar
            def _(scalar):
                for tt in range(NTT):
                    sl = tt % 2
                    scalar.wait_ge(SM["s_mvB"], NTT + tt + 1)
                    scalar.activation(out=stat[:, 16 + 8 * sl + 2: 16 + 8 * sl + 3],
                                      in_=stat[:, 16 + 8 * sl + 1: 16 + 8 * sl + 2],
                                      func=AF.Sqrt, bias=epsv).then_inc(SM["s_sdB"], 1)
                    scalar.wait_ge(SM["s_rsB"], NTT + tt + 1)
                    scalar.wait_ge(SM["s_cao"], 16 * (NTT + tt + 1))
                    if tt >= 1:
                        scalar.wait_ge(SM["s_x2"], NTT + tt)
                    scalar.activation(out=C_tmp, in_=C_ao(sl), func=AF.Identity,
                                      bias=stat[:, 16 + 8 * sl + 4: 16 + 8 * sl + 5],
                                      scale=stat[:, 16 + 8 * sl + 3: 16 + 8 * sl + 4]
                                      ).then_inc(SM["s_tmp"], 1)

    return nc


def _stage_inputs(L0_attn_hidden_states, L0_attn_Wq, L0_attn_Wkv, L0_attn_Wo,
                  L0_ffn_W, L0_ffn_bias, L0_ffn_A, L0_ffn_Bf, L0_ffn_coeffs,
                  L0_ffn_scales):
    f32 = np.float32
    X = np.asarray(L0_attn_hidden_states, f32)
    Wq = np.asarray(L0_attn_Wq, f32)
    Wkv = np.asarray(L0_attn_Wkv, f32)
    Wo = np.asarray(L0_attn_Wo, f32)
    W = np.asarray(L0_ffn_W, f32)
    bias = np.asarray(L0_ffn_bias, f32)
    A = np.asarray(L0_ffn_A, f32)
    Bf = np.asarray(L0_ffn_Bf, f32)
    coeffs = np.asarray(L0_ffn_coeffs, f32)
    scales = np.asarray(L0_ffn_scales, f32)

    wq_l = np.ascontiguousarray(
        Wq.reshape(NCH, P, NH, HD).transpose(2, 1, 0, 3).reshape(NH, P, D))
    wkv_l = np.ascontiguousarray(
        Wkv.reshape(NCH, P, 256).transpose(1, 0, 2).reshape(P, NCH * 256))
    wo_l = np.ascontiguousarray(
        Wo.reshape(NCH, P, 8, QB).transpose(2, 1, 0, 3).reshape(8, P, NCH * QB))
    WT = np.ascontiguousarray(W.T)
    wt_l = np.ascontiguousarray(
        WT.reshape(NCH, P, 8, QB).transpose(2, 1, 0, 3).reshape(8, P, NCH * QB))
    AT = np.ascontiguousarray(A.reshape(256, D).T)
    at_l = np.ascontiguousarray(
        AT.reshape(NCH, P, 256).transpose(1, 0, 2).reshape(P, NCH * 256))
    BfT = np.ascontiguousarray(Bf.transpose(0, 2, 1).reshape(256, D))
    bft_l = np.ascontiguousarray(
        BfT.reshape(2, P, 8, QB).transpose(2, 1, 0, 3).reshape(8, P, 2 * QB))

    wsel_np = np.zeros((16, 2 * P), f32)
    for c2 in range(2):
        for e8 in range(8):
            wsel_np[8 * c2 + e8, c2 * P + e8 * 16: c2 * P + (e8 + 1) * 16] = 1.0

    slopes = _alibi_slopes(NH)
    pvec = np.arange(P, dtype=np.float64)
    jvec = np.arange(QB, dtype=np.float64)

    dmat_np = np.zeros((P, 4 * QB), f32)
    for o in range(4):
        rel = (pvec[:, None] + 128 * o) - jvec[None, :]
        m = np.where(pvec[:, None] + 128 * o <= jvec[None, :], rel, NEG)
        dmat_np[:, o * QB:(o + 1) * QB] = m.astype(f32)

    in_maps = []
    for c in range(8):
        b, j = divmod(c, 4)
        g = j
        heads = [8 * g + u for u in range(NU)]
        skm = np.zeros((P, NU * NQB * 32), f32)
        er = np.zeros((NU, QB), f32)
        dsl = np.zeros((P, NU), f32)
        for u, h in enumerate(heads):
            sl = slopes[h]
            er[u] = np.exp(-sl * jvec).astype(f32)
            dsl[:, u] = np.float32(sl / SC)
            for q in range(NQB):
                for kt in range(32):
                    skm[:, (u * NQB + q) * 32 + kt] = np.exp(np.minimum(
                        sl * (kt * 128 + pvec - q * QB), 0.0)).astype(f32)
        qx = np.zeros((P, NU * 4), np.int32)
        for u, h in enumerate(heads):
            hp = POS[h] if POS[h] < 16 else POS[h] - 16
            for r in range(4):
                qx[:, u * 4 + r] = r * 16 * P + hp * P + np.arange(P)
        ox = np.zeros((P, NCH), np.int32)
        for cch in range(NCH):
            r, u = divmod(cch, NU)
            ox[:, cch] = r * (NU * P * 4) + (u * P + np.arange(P)) * 4 + j
        in_maps.append({
            "x_own": np.ascontiguousarray(X[b, j * TOK:(j + 1) * TOK, :]),
            "wq": wq_l, "wkv": wkv_l, "wo": wo_l, "wt": wt_l,
            "at_in": at_l, "bft": bft_l,
            "bias_in": bias.reshape(1, D),
            "coeffs_in": coeffs.reshape(16, 1),
            "scales_in": scales.reshape(16, 1),
            "wsel": wsel_np,
            "skmat": skm, "dmat": dmat_np, "erow": er, "dslopes": dsl,
            "qidx": qx, "oidx": ox,
        })
    return in_maps


_NC_CACHE = {}


def kernel(**inputs):
    in_maps = _stage_inputs(**inputs)
    if "nc" not in _NC_CACHE:
        _NC_CACHE["nc"] = build_program(debug=False)
    nc = _NC_CACHE["nc"]
    res = run_bass_kernel_spmd(nc, in_maps, core_ids=list(range(8)))
    out = np.empty((2, S, D), np.float32)
    for c in range(8):
        b, j = divmod(c, 4)
        out[b, j * TOK:(j + 1) * TOK, :] = res.results[c]["y_out"]
    return out
